# version 08 (final)
# speedup vs baseline: 1.5664x; 1.0077x over previous
import os as _os
import numpy as np
import ml_dtypes
from contextlib import ExitStack
import concourse.bass as bass
import concourse.mybir as mybir
from concourse.bass_utils import run_bass_kernel_spmd

F32 = mybir.dt.float32
BF16 = mybir.dt.bfloat16
AF = mybir.ActivationFunctionType
U8 = mybir.dt.uint8
ALU = mybir.AluOpType

BF = ml_dtypes.bfloat16
B, T, D = 2, 8192, 1024
CT = 2048
NCORE = 8
DT = 128
NDT = D // DT
EPS_LN = 1e-5
EPS_ATT = 1e-6
PC = 1024
NSC = PC // 512

_s_m = np.float32(np.float32(-10000.0) / np.float32(10001.0))
_w_m = np.float32((_s_m + np.float32(1.0)) * np.float32(0.5))
LN_C0 = float(np.log(np.float64(_w_m)))


def bfc(x):
    return np.ascontiguousarray(np.asarray(x).astype(BF))


def f32c(x):
    return np.ascontiguousarray(np.asarray(x), dtype=np.float32)



ENGINES = ("sync", "gpsimd", "vector", "scalar", "tensor")


ENGINE_SEM = {"vector": "vd", "scalar": "sd", "tensor": "td", "gpsimd": "gd"}


class Prog:
    def __init__(self, name):
        self.nc = bass.Bass("TRN2", target_bir_lowering=False)
        self.es = ExitStack()
        self.queues = {e: [] for e in ENGINES}
        self.ctr = {}
        self.sems = {}
        self.consts = {}
        self._cpool = None
        self.name = name
        self.auto_self_wait = {"vector", "scalar", "gpsimd"}

    def sb(self, name, shape, dt=F32):
        return self.es.enter_context(self.nc.sbuf_tensor(name, shape, dt))

    def psum(self, name, shape, dt=F32):
        return self.es.enter_context(self.nc.psum_tensor(name, shape, dt))

    def sem(self, name):
        s = self.es.enter_context(self.nc.semaphore(name))
        self.sems[name] = s
        self.ctr[name] = 0
        return s

    def register_consts(self, values):
        vals = sorted(set(float(v) for v in values))
        self._cpool = self.sb("cpool", [128, max(len(vals), 1)])
        for i, v in enumerate(vals):
            self.nc.const_aps.register(F32, v, self._cpool[:, i:i + 1])
            self.consts[v] = i

    def op(self, engine, fn, inc=None):
        if engine in self.auto_self_wait:
            own = ENGINE_SEM.get(engine)
            if own in self.ctr and self.ctr[own] > 0:
                v = self.ctr[own]
                s = self.sems[own]
                self.queues[engine].append(lambda e, s=s, v=v: e.wait_ge(s, v))
        if inc is None:
            self.queues[engine].append(fn)
            return None
        sem_name, k = inc
        self.ctr[sem_name] += k
        s = self.sems[sem_name]
        self.queues[engine].append(lambda e: fn(e).then_inc(s, k))
        return self.ctr[sem_name]

    def self_wait(self, engine, sem_name):
        v = self.ctr[sem_name]
        s = self.sems[sem_name]
        self.queues[engine].append(lambda e: e.wait_ge(s, v))

    def wait(self, engine, sem_name, value=None):
        v = self.ctr[sem_name] if value is None else value
        s = self.sems[sem_name]
        self.queues[engine].append(lambda e: e.wait_ge(s, v))
        return v

    def emit_const_memsets(self, engine, inc):
        sem_name, k = inc
        self.ctr[sem_name] += k
        s = self.sems[sem_name]
        items = list(self.consts.items())
        pool = self._cpool

        def fn(e):
            insts = [e.memset(pool[:, i:i + 1], v) for v, i in items]
            insts[-1].then_inc(s, k)
        self.queues[engine].append(fn)

    def finalize(self):
        with self.es:
            block = self.es.enter_context(self.nc.Block())
            for ename in ENGINES:
                q = self.queues[ename]
                getattr(block, ename)(lambda eng, q=q: [fn(eng) for fn in q] and None)
        return self.nc


class DmaStream:

    def __init__(self, P, name, queue, nbuf=1):
        self.P = P
        self.queue = queue
        self.nbuf = nbuf
        self.sems = []
        self.cnt = [0] * nbuf
        for i in range(nbuf):
            self.sems.append(P.sem(f"{name}{i}"))
            P.ctr[f"{name}{i}"] = 0
        self.names = [f"{name}{i}" for i in range(nbuf)]
        self.n = 0

    def issue(self, fn, parity=None):
        p = (self.n % self.nbuf) if parity is None else (parity % self.nbuf)
        self.n += 1
        nm = self.names[p]
        self.P.ctr[nm] += 16
        s = self.sems[p]
        self.P.queues[self.queue].append(lambda e: fn(e).then_inc(s, 16))
        return (p, self.P.ctr[nm])

    def wait(self, engine, token):
        p, v = token
        s = self.sems[p]
        self.P.queues[engine].append(lambda e: e.wait_ge(s, v))



def host_prep(inputs):
    hp = {}
    x = np.asarray(inputs["L0_attn_x"])
    hp["xT"] = [f32c(x[c // 4, (c % 4) * CT:(c % 4 + 1) * CT, :].T) for c in range(NCORE)]

    q = np.asarray(inputs["L0_ffn_qweights"])
    hp["qT"] = f32c(np.transpose(q, (0, 2, 1)))
    sc = np.asarray(inputs["L0_ffn_scales"])
    hp["srepT"] = f32c(np.transpose(np.repeat(sc, 16, axis=2), (0, 2, 1)))
    hp["fbias"] = bfc(inputs["L0_ffn_biases"])
    hp["laT"] = bfc(np.transpose(np.asarray(inputs["L0_ffn_lora_a"]), (0, 2, 1)))
    hp["lbT"] = bfc(np.transpose(np.asarray(inputs["L0_ffn_lora_b"]), (0, 2, 1)))
    g = f32c(inputs["L0_ffn_ln_gamma"]).reshape(3, NDT, DT)
    b_ = f32c(inputs["L0_ffn_ln_beta"]).reshape(3, NDT, DT)
    hp["ln_g_cols"] = f32c(np.transpose(g, (2, 0, 1)).reshape(DT, 3 * NDT))
    hp["ln_b_cols"] = f32c(np.transpose(b_, (2, 0, 1)).reshape(DT, 3 * NDT))

    for nm in ["Wq", "Wk", "Wv", "Wo"]:
        hp[nm + "T"] = bfc(np.asarray(inputs[f"L0_attn_{nm}"]).T)
    hp["slopes"] = f32c(inputs["L0_attn_alibi_slopes"]).reshape(16, 1)
    hp["power"] = f32c(inputs["L0_attn_power"]).reshape(1, 1)

    for nm in ["cn_w1", "cn_w2", "cn_w3", "cn_w4", "gc_w1", "gc_w2", "gc_w3"]:
        hp[nm + "T"] = bfc(inputs[f"L1_ffn_{nm}"])

    def cols(v, nt):
        v = f32c(v).reshape(nt, DT)
        return f32c(v.T)
    hp["cn_w4c"] = bfc(np.asarray(inputs["L1_ffn_cn_w4"], np.float32).reshape(2, DT).T)
    hp["gc_w3c"] = bfc(np.asarray(inputs["L1_ffn_gc_w3"], np.float32).reshape(2, DT).T)
    hp["cn_b1c"] = cols(inputs["L1_ffn_cn_b1"], NDT)
    hp["cn_b2c"] = cols(inputs["L1_ffn_cn_b2"], 4)
    hp["cn_b3c"] = cols(inputs["L1_ffn_cn_b3"], 2)
    hp["cn_b4c"] = f32c(inputs["L1_ffn_cn_b4"]).reshape(1, 1)
    hp["gc_b1c"] = cols(inputs["L1_ffn_gc_b1"], 4)
    hp["gc_b2c"] = cols(inputs["L1_ffn_gc_b2"], 2)
    hp["gc_b3c"] = f32c(inputs["L1_ffn_gc_b3"]).reshape(1, 1)
    hp["ar"] = f32c(inputs["L1_ffn_adaptive_range"]).reshape(1, 1)

    hp["m_ln_w_c"] = cols(inputs["L1_attn_ln_w"], NDT)
    hp["m_ln_b_c"] = cols(inputs["L1_attn_ln_b"], NDT)
    hp["WinT"] = bfc(np.asarray(inputs["L1_attn_W_in"]).T)
    hp["conv_w_c"] = f32c(np.concatenate(
        [np.asarray(inputs["L1_attn_conv_w"]), np.zeros((NDT * DT - 2176, 4), np.float32)],
        axis=0).reshape(17, DT, 4).transpose(1, 0, 2).reshape(DT, 17 * 4)) if False else f32c(
        np.vstack([np.asarray(inputs["L1_attn_conv_w"], np.float32)]).reshape(17, DT, 4).transpose(1, 0, 2).reshape(DT, 68))
    hp["conv_b_c"] = f32c(np.asarray(inputs["L1_attn_conv_b"], np.float32).reshape(17, DT).T)
    hp["dt_bias_c"] = f32c(inputs["L1_attn_dt_bias"]).reshape(32, 1)
    hp["A_log_c"] = f32c(inputs["L1_attn_A_log"]).reshape(32, 1)
    hp["Dm_c"] = f32c(inputs["L1_attn_D"]).reshape(32, 1)
    hp["norm_w_c"] = f32c(np.asarray(inputs["L1_attn_norm_w"], np.float32).reshape(16, DT).T)
    hp["WoutT"] = bfc(np.asarray(inputs["L1_attn_W_out"]).T)

    cw = np.asarray(inputs["L1_attn_conv_w"], np.float32)
    cb = np.asarray(inputs["L1_attn_conv_b"], np.float32)
    dtb = f32c(inputs["L1_attn_dt_bias"]).reshape(32)
    alog = f32c(inputs["L1_attn_A_log"]).reshape(32)
    dm = f32c(inputs["L1_attn_D"]).reshape(32)
    hp["convdx"] = []
    hp["convbx"] = []
    hp["dtbias_t"] = []
    hp["alog_rep"] = []
    hp["alog8r"] = []
    hp["dcol8"] = []
    for hg in range(4):
        hp["convdx"].append(bfc(np.stack(
            [np.diag(cw[512 * hg + g * 128: 512 * hg + (g + 1) * 128, k])
             for g in range(4) for k in range(4)])))
        hp["convbx"].append(f32c(cb[512 * hg: 512 * (hg + 1)].reshape(4, 128).T))
        hp["dtbias_t"].append(f32c(np.broadcast_to(
            np.tile(dtb[8 * hg: 8 * hg + 8], 64)[None, :], (128, 512))))
        hp["alog_rep"].append(f32c(np.broadcast_to(
            alog[8 * hg: 8 * hg + 8][None, :], (128, 8))))
        hp["alog8r"].append(f32c(np.broadcast_to(
            alog[8 * hg: 8 * hg + 8][:, None], (8, 128))))
        hp["dcol8"].append(f32c(np.broadcast_to(
            dm[8 * hg: 8 * hg + 8][None, :], (128, 8))))
    hp["convdb"] = bfc(np.stack([np.diag(cw[2048 + g * 64: 2048 + (g + 1) * 64, k])
                                 for g, k in [(0, 0), (0, 1), (0, 2), (0, 3)]]))
    hp["convdc"] = bfc(np.stack([np.diag(cw[2112 + g * 64: 2112 + (g + 1) * 64, k])
                                 for g, k in [(0, 0), (0, 1), (0, 2), (0, 3)]]))
    hp["convbb"] = f32c(cb[2048:2112].reshape(64, 1))
    hp["convbc"] = f32c(cb[2112:2176].reshape(64, 1))
    hp["triu128"] = bfc(np.triu(np.ones((128, 128), np.float32)))
    hp["iden128"] = bfc(np.eye(128, dtype=np.float32))
    hp["i8t"] = bfc(np.tile(np.eye(8, dtype=np.float32), 64))

    hp["kidx16"] = [f32c(np.broadcast_to(
        np.arange((c % 4) * CT, (c % 4 + 1) * CT, dtype=np.float32)[None, :],
        (16, CT))) for c in range(NCORE)]
    hp["ones16"] = f32c(np.ones((16, CT), np.float32))
    return hp




def emit_panel_ln(ctx, src_fn, cast_fn, dst_fn, dstb_fn, eps, gamma_ap=None,
                  beta_ap=None, act_func=None):
    P = ctx["P"]
    sqs, rows, rowb = ctx["sqs"], ctx["rows"], ctx["rowb"]
    ones_col, ones_row = ctx["ones_col"], ctx["ones_row"]
    ps_row, ps_bc = ctx["ps_row"], ctx["ps_bc"]
    NSC_ = ctx.get("NSC", NSC)
    for j in range(NDT):
        P.op("vector", lambda e, j=j: e.tensor_copy(cast_fn(j), src_fn(j)),
             inc=("vd", 1))
    vd_c = P.ctr["vd"]
    P.wait("scalar", "vd", vd_c)
    P.wait("tensor", "vd", vd_c)
    for c in range(NSC_):
        for j in range(NDT):
            P.op("scalar", lambda e, j=j, c=c: e.activation(
                sqs[:, (j % 2) * 512:(j % 2) * 512 + 512],
                src_fn(j)[:, c * 512:(c + 1) * 512], AF.Square),
                inc=("sd", 1))
            sd_j = P.ctr["sd"]
            P.wait("tensor", "sd", sd_j)
            P.op("tensor", lambda e, j=j, c=c: e.matmul(
                ps_row[0:1, 0:512], ones_col[:, :],
                cast_fn(j)[:, c * 512:(c + 1) * 512],
                start=(j == 0), stop=(j == NDT - 1)), inc=("td", 1))
            P.op("tensor", lambda e, j=j, c=c: e.matmul(
                ps_row[32:33, 0:512], ones_col[:, :],
                sqs[:, (j % 2) * 512:(j % 2) * 512 + 512],
                start=(j == 0), stop=(j == NDT - 1)), inc=("td", 1))
            if j < NDT - 1:
                P.wait("scalar", "td", P.ctr["td"])
        td_st = P.ctr["td"]
        R = 512
        P.wait("vector", "td", td_st)
        P.op("vector", lambda e: e.tensor_scalar(
            rows[0:1, 0:R], ps_row[0:1, 0:R], 1.0 / 1024.0, None, ALU.mult),
            inc=("vd", 1))
        P.op("vector", lambda e: e.tensor_tensor(
            rows[0:1, R:2 * R], rows[0:1, 0:R], rows[0:1, 0:R], ALU.mult),
            inc=("vd", 1))
        P.op("vector", lambda e: e.scalar_tensor_tensor(
            rows[0:1, R:2 * R], ps_row[32:33, 0:R], 1.0 / 1024.0,
            rows[0:1, R:2 * R], ALU.mult, ALU.subtract), inc=("vd", 1))
        vd_var = P.ctr["vd"]
        P.wait("scalar", "vd", vd_var)
        P.op("scalar", lambda e: e.activation(
            rows[0:1, R:2 * R], rows[0:1, R:2 * R], AF.Ln, bias=float(eps)),
            inc=("sd", 1))
        P.op("scalar", lambda e: e.activation(
            rows[0:1, R:2 * R], rows[0:1, R:2 * R], AF.Exp, scale=-0.5),
            inc=("sd", 1))
        sd_r = P.ctr["sd"]
        P.wait("vector", "sd", sd_r)
        P.op("vector", lambda e: e.tensor_tensor(
            rows[0:1, 0:R], rows[0:1, 0:R], rows[0:1, R:2 * R],
            ALU.mult), inc=("vd", 1))
        P.op("vector", lambda e: e.tensor_copy(
            rowb[0:1, 0:R], rows[0:1, R:2 * R]), inc=("vd", 1))
        P.op("vector", lambda e: e.tensor_copy(
            rowb[0:1, R:2 * R], rows[0:1, 0:R]), inc=("vd", 1))
        vd_rows = P.ctr["vd"]
        P.wait("tensor", "vd", vd_rows)
        P.op("tensor", lambda e: e.matmul(
            ps_bc[:, 0:512], ones_row[0:1, 0:DT], rowb[0:1, 0:R],
            start=True, stop=True), inc=("td", 1))
        P.op("tensor", lambda e: e.matmul(
            ps_bc[:, 512:1024], ones_row[0:1, 0:DT], rowb[0:1, R:2 * R],
            start=True, stop=True), inc=("td", 1))
        td_bc = P.ctr["td"]
        P.wait("vector", "td", td_bc)
        if gamma_ap is not None:
            P.wait("scalar", "td", td_bc)
        for j in range(NDT):
            P.op("vector", lambda e, j=j, c=c: e.tensor_tensor(
                dst_fn(j)[:, c * 512:(c + 1) * 512],
                src_fn(j)[:, c * 512:(c + 1) * 512],
                ps_bc[:, 0:512], ALU.mult), inc=("vd", 1))
            P.op("vector", lambda e, j=j, c=c: e.tensor_tensor(
                dst_fn(j)[:, c * 512:(c + 1) * 512],
                dst_fn(j)[:, c * 512:(c + 1) * 512],
                ps_bc[:, 512:1024], ALU.subtract), inc=("vd", 1))
            vd_ap = P.ctr["vd"]
            if gamma_ap is not None and act_func is AF.Copy:
                P.op("vector", lambda e, j=j, c=c: e.tensor_scalar(
                    dstb_fn(j)[:, c * 512:(c + 1) * 512],
                    dst_fn(j)[:, c * 512:(c + 1) * 512],
                    gamma_ap(j), beta_ap(j), ALU.mult, ALU.add),
                    inc=("vd", 1))
            elif gamma_ap is not None:
                P.wait("scalar", "vd", vd_ap)
                P.op("scalar", lambda e, j=j, c=c: e.activation(
                    dstb_fn(j)[:, c * 512:(c + 1) * 512],
                    dst_fn(j)[:, c * 512:(c + 1) * 512],
                    act_func, bias=beta_ap(j), scale=gamma_ap(j)),
                    inc=("sd", 1))
            else:
                P.op("vector", lambda e, j=j, c=c: e.tensor_copy(
                    dstb_fn(j)[:, c * 512:(c + 1) * 512],
                    dst_fn(j)[:, c * 512:(c + 1) * 512]), inc=("vd", 1))
        P.wait("tensor", "vd", P.ctr["vd"])
        if gamma_ap is not None:
            P.wait("tensor", "sd", P.ctr["sd"])


def emit_qlora_mm(ctx, widx, rhs_fn, consumer):
    P = ctx["P"]
    ps_r, ps_y, lorar = ctx["ps_r"], ctx["ps_y"], ctx["lorar"]
    wmm, la_sb, lb_sb, bias_sb = (ctx["wmm"], ctx["la_sb"], ctx["lb_sb"],
                                  ctx["bias_sb"])
    ones_row = ctx["ones_row"]
    S_w, S_lb = ctx["S_w"], ctx["S_lb"]
    wdeq_ap, lbT_ap, fbias_ap = ctx["wdeq_ap"], ctx["lbT_ap"], ctx["fbias_ap"]
    NSC_ = ctx.get("NSC", NSC)
    P.wait("sync", "td", P.ctr["td"])
    S_lb.issue(lambda e: e.dma_start(out=lb_sb[:, :], in_=lbT_ap[widx]))
    tok_lb = S_lb.issue(lambda e: e.dma_start(
        out=bias_sb[:, :], in_=fbias_ap[widx:widx + 1, :]))
    S_lb.wait("tensor", tok_lb)
    for c in range(NSC_):
        for j in range(NDT):
            P.op("tensor", lambda e, c=c, j=j: e.matmul(
                ps_r[:, c * 512:(c + 1) * 512],
                la_sb[:, (widx * NDT + j) * 32:(widx * NDT + j + 1) * 32],
                rhs_fn(j)[:, c * 512:(c + 1) * 512],
                start=(j == 0), stop=(j == NDT - 1)), inc=("td", 1))
    td_r = P.ctr["td"]
    P.wait("vector", "td", td_r)
    P.op("vector", lambda e: e.tensor_copy(lorar[:, 0:PC], ps_r[:, 0:PC]),
         inc=("vd", 1))
    vd_r = P.ctr["vd"]
    P.wait("tensor", "vd", vd_r)
    td_w = []
    for i in range(NDT):
        buf = i % 2
        if i >= 2:
            P.wait("sync", "td", td_w[i - 2])
        tw = S_w.issue(lambda e, i=i, buf=buf: e.dma_start(
            out=wmm[:, buf * D:(buf + 1) * D].rearrange(
                "p (j o) -> p j o", j=NDT),
            in_=wdeq_ap[widx, :, i * DT:(i + 1) * DT].rearrange(
                "(j p) o -> p j o", p=DT)), parity=buf)
        S_w.wait("tensor", tw)
        for c in range(NSC_):
            for j in range(NDT):
                P.op("tensor", lambda e, i=i, c=c, j=j, buf=buf: e.matmul(
                    ps_y[:, c * 512:(c + 1) * 512],
                    wmm[:, buf * D + j * DT: buf * D + (j + 1) * DT],
                    rhs_fn(j)[:, c * 512:(c + 1) * 512],
                    start=(j == 0), stop=False), inc=("td", 1))
            P.op("tensor", lambda e, i=i, c=c: e.matmul(
                ps_y[:, c * 512:(c + 1) * 512],
                lb_sb[:, i * DT:(i + 1) * DT],
                lorar[:, c * 512:(c + 1) * 512], start=False, stop=False),
                inc=("td", 1))
            P.op("tensor", lambda e, i=i, c=c: e.matmul(
                ps_y[:, c * 512:(c + 1) * 512],
                bias_sb[0:1, i * DT:(i + 1) * DT],
                ones_row[0:1, c * 512:(c + 1) * 512], start=False, stop=True),
                inc=("td", 1))
        td_w.append(P.ctr["td"])
        consumer(i)


def build_launch_A(dbg=False):
    P = Prog("A")
    nc = P.nc

    xT_d = nc.dram_tensor("xT", [D, CT], F32, kind="ExternalInput").ap()
    qT_d = nc.dram_tensor("qT", [6, D, D], F32, kind="ExternalInput").ap()
    srepT_d = nc.dram_tensor("srepT", [6, D, D], F32, kind="ExternalInput").ap()
    fbias_d = nc.dram_tensor("fbias", [6, D], BF16, kind="ExternalInput").ap()
    laT_d = nc.dram_tensor("laT", [6, D, 32], BF16, kind="ExternalInput").ap()
    lbT_d = nc.dram_tensor("lbT", [6, 32, D], BF16, kind="ExternalInput").ap()
    lng_d = nc.dram_tensor("ln_g_cols", [DT, 3 * NDT], F32, kind="ExternalInput").ap()
    lnb_d = nc.dram_tensor("ln_b_cols", [DT, 3 * NDT], F32, kind="ExternalInput").ap()
    WqT_d = nc.dram_tensor("WqT", [D, D], BF16, kind="ExternalInput").ap()
    WkT_d = nc.dram_tensor("WkT", [D, D], BF16, kind="ExternalInput").ap()
    WvT_d = nc.dram_tensor("WvT", [D, D], BF16, kind="ExternalInput").ap()
    slopes_d = nc.dram_tensor("slopes", [16, 1], F32, kind="ExternalInput").ap()
    kidx16_d = nc.dram_tensor("kidx16", [16, CT], F32, kind="ExternalInput").ap()
    ones16_d = nc.dram_tensor("ones16", [16, CT], F32, kind="ExternalInput").ap()

    x1T_o = nc.dram_tensor("x1T", [D, CT], F32, kind="ExternalOutput").ap()
    Q_o = nc.dram_tensor("Q", [D, CT], BF16, kind="ExternalOutput").ap()
    K_o = nc.dram_tensor("K", [D, CT], BF16, kind="ExternalOutput").ap()
    augK_o = nc.dram_tensor("augK", [32, CT], BF16, kind="ExternalOutput").ap()
    V_o = nc.dram_tensor("V", [CT, D], BF16, kind="ExternalOutput").ap()
    wdeq_o = nc.dram_tensor("wdeq", [6, D, D], BF16, kind="ExternalOutput").ap()
    if dbg:
        dbg1_o = nc.dram_tensor("dbg1", [D, PC], F32, kind="ExternalOutput").ap()
        dbg2_o = nc.dram_tensor("dbg2", [D, PC], F32, kind="ExternalOutput").ap()
        dbg3_o = nc.dram_tensor("dbg3", [2, 3, D, PC], F32, kind="ExternalOutput").ap()
        dbg4_o = nc.dram_tensor("dbg4", [D, PC], BF16, kind="ExternalOutput").ap()

    P.register_consts([0.5, 2.0, -1.0, 1.0 / 1024.0, EPS_LN, -0.5, 1.0 / 7.5])

    xt = P.sb("xt", [DT, NDT * PC])
    h = P.sb("h", [DT, NDT * PC])
    hb = P.sb("hb", [DT, NDT * PC], BF16)
    y = P.sb("y", [DT, NDT * PC])
    x2b = P.sb("x2b", [DT, NDT * PC], BF16)
    sqs = P.sb("sqs", [DT, 2 * 512], BF16)
    xhb = P.sb("xhb", [DT, NDT * CT], BF16)
    wq_f = P.sb("wq_f", [DT, 2 * D])
    ws_f = P.sb("ws_f", [DT, D])
    wd_b = P.sb("wd_b", [DT, 2 * D], BF16)
    wmm = P.sb("wmm", [DT, 2 * D], BF16)
    la_sb = P.sb("la_sb", [DT, 6 * NDT * 32], BF16)
    lb_sb = P.sb("lb_sb", [32, D], BF16)
    bias_sb = P.sb("bias_sb", [1, D], BF16)
    lng_sb = P.sb("lng_sb", [DT, 3 * NDT])
    lnb_sb = P.sb("lnb_sb", [DT, 3 * NDT])
    lorar = P.sb("lorar", [32, PC], BF16)
    rows = P.sb("rows", [1, 2 * 512])
    rowb = P.sb("rowb", [1, 2 * PC], BF16)
    ones_col = P.sb("ones_col", [DT, 1], BF16)
    ones_row = P.sb("ones_row", [1, PC], BF16)
    qkev = P.sb("qkev", [DT, CT], BF16)
    slopes_sb = P.sb("slopes_sb", [16, 1])

    ps_y = P.psum("ps_y", [DT, PC])
    ps_bc = P.psum("ps_bc", [DT, PC])
    ps_row = P.psum("ps_row", [33, 512])
    ps_r = P.psum("ps_r", [32, PC])

    for s in ["vd", "sd", "td", "gd"]:
        P.sem(s)
    S_param = DmaStream(P, "lp", "sync", 1)
    S_q = DmaStream(P, "lq", "sync", 2)
    S_s = DmaStream(P, "ls", "sync", 1)
    S_w = DmaStream(P, "lw", "sync", 2)
    S_x = DmaStream(P, "lx", "sync", 1)
    S_lb = DmaStream(P, "llb", "sync", 1)
    S_v = DmaStream(P, "lv", "sync", 1)
    S_m = DmaStream(P, "lm", "sync", 1)
    S_ox = DmaStream(P, "gx", "gpsimd", 1)
    S_owd = DmaStream(P, "gw", "gpsimd", 2)
    S_oq = DmaStream(P, "gq", "gpsimd", 1)
    S_ov = DmaStream(P, "gv", "gpsimd", 2)
    S_oa = DmaStream(P, "ga", "gpsimd", 1)
    S_dbg = DmaStream(P, "gdbg", "gpsimd", 1)

    P.emit_const_memsets("gpsimd", ("gd", 1))
    gd_init = P.ctr["gd"]

    for j in range(NDT):
        S_param.issue(lambda e, j=j: e.dma_start(
            out=la_sb[:, :].rearrange("p (w j r) -> p w j r", w=6, j=NDT)[:, :, j, :],
            in_=laT_d[:, j * DT:(j + 1) * DT, :].rearrange("w p r -> p w r")))
    S_param.issue(lambda e: e.dma_start(out=lng_sb[:, :], in_=lng_d))
    S_param.issue(lambda e: e.dma_start(out=lnb_sb[:, :], in_=lnb_d))
    tok_params = S_param.issue(lambda e: e.dma_start(out=slopes_sb[:, :], in_=slopes_d))
    P.wait("vector", "gd", gd_init)
    P.op("vector", lambda e: e.memset(ones_col[:, :], 1.0), inc=("vd", 1))
    P.op("vector", lambda e: e.memset(ones_row[:, :], 1.0), inc=("vd", 1))
    vd_ones = P.ctr["vd"]

    ndq = 0
    vd_dq = {}
    tok_wd = {}
    for w in range(6):
        for j in range(NDT):
            buf = ndq % 2
            if ndq >= 2:
                P.wait("sync", "vd", vd_dq[ndq - 2])
            tq = S_q.issue(lambda e, w=w, j=j, buf=buf: e.dma_start(
                out=wq_f[:, buf * D:(buf + 1) * D],
                in_=qT_d[w, j * DT:(j + 1) * DT, :]), parity=buf)
            if ndq >= 1:
                P.wait("sync", "vd", vd_dq[ndq - 1])
            ts = S_s.issue(lambda e, w=w, j=j: e.dma_start(
                out=ws_f[:, :], in_=srepT_d[w, j * DT:(j + 1) * DT, :]))
            S_q.wait("scalar", tq)
            P.op("scalar", lambda e, buf=buf: e.activation(
                wq_f[:, buf * D:(buf + 1) * D], wq_f[:, buf * D:(buf + 1) * D],
                AF.Copy, bias=-1.0, scale=1.0 / 7.5), inc=("sd", 1))
            sd_now = P.ctr["sd"]
            P.wait("vector", "sd", sd_now)
            S_s.wait("vector", ts)
            if ndq >= 2:
                S_owd.wait("vector", tok_wd[ndq - 2])
            P.op("vector", lambda e, buf=buf: e.tensor_tensor(
                wd_b[:, buf * D:(buf + 1) * D], wq_f[:, buf * D:(buf + 1) * D],
                ws_f[:, :], ALU.mult), inc=("vd", 1))
            vd_dq[ndq] = P.ctr["vd"]
            P.wait("gpsimd", "vd", vd_dq[ndq])
            tok_wd[ndq] = S_owd.issue(lambda e, w=w, j=j, buf=buf: e.dma_start(
                out=wdeq_o[w, j * DT:(j + 1) * DT, :],
                in_=wd_b[:, buf * D:(buf + 1) * D]), parity=buf)
            ndq += 1
    tok_wd_last0 = tok_wd[ndq - 2]
    tok_wd_last1 = tok_wd[ndq - 1]

    ctx = dict(P=P, sqs=sqs, rows=rows, rowb=rowb, ones_col=ones_col,
               ones_row=ones_row, ps_row=ps_row, ps_bc=ps_bc,
               ps_r=ps_r, ps_y=ps_y, lorar=lorar, wmm=wmm, la_sb=la_sb,
               lb_sb=lb_sb, bias_sb=bias_sb, S_w=S_w, S_lb=S_lb,
               wdeq_ap=wdeq_o, lbT_ap=lbT_d, fbias_ap=fbias_d)

    def panel_ln2(src_fn, cast_fn, dst_fn, dstb_fn, eps, gamma_ap=None,
                  beta_ap=None, act_func=None):
        emit_panel_ln(ctx, src_fn, cast_fn, dst_fn, dstb_fn, eps, gamma_ap,
                      beta_ap, act_func)

    def qlora_mm(widx, rhs_fn, consumer):
        emit_qlora_mm(ctx, widx, rhs_fn, consumer)

    S_owd.wait("sync", tok_wd_last0)
    S_owd.wait("sync", tok_wd_last1)
    for ch in range(2):
        tok0 = ch * PC
        if ch > 0:
            P.wait("sync", "vd", P.ctr["vd"])
            P.wait("sync", "sd", P.ctr["sd"])
            P.wait("sync", "gd", P.ctr["gd"])
            S_ox.wait("sync", tox1[-1])
        tox = [S_x.issue(lambda e, j=j, tok0=tok0: e.dma_start(
            out=xt[:, j * PC:(j + 1) * PC],
            in_=xT_d[j * DT:(j + 1) * DT, tok0:tok0 + PC])) for j in range(NDT)]
        for eng in ("vector", "scalar", "tensor"):
            S_x.wait(eng, tox[-1])
            S_param.wait(eng, tok_params)
        P.wait("vector", "vd", vd_ones)

        sl = lambda buf, j: buf[:, j * PC:(j + 1) * PC]
        panel_ln2(lambda j: sl(xt, j), lambda j: sl(x2b, j),
                  lambda j: sl(h, j), lambda j: sl(hb, j), EPS_LN)
        if dbg and ch == 0:
            P.wait("gpsimd", "vd", P.ctr["vd"])
            for j in range(NDT):
                S_dbg.issue(lambda e, j=j: e.dma_start(
                    out=dbg1_o[j * DT:(j + 1) * DT, :],
                    in_=h[:, j * PC:(j + 1) * PC]))
            P.wait("vector", "gd0_dbg_guard", 0) if False else None

        for blk in range(3):
            def consume_mm1(i):
                P.wait("scalar", "td", P.ctr["td"])
                P.op("scalar", lambda e, i=i: e.activation(
                    y[:, i * PC:(i + 1) * PC], ps_y[:, 0:PC], AF.Copy),
                    inc=("sd", 1))
                P.wait("tensor", "sd", P.ctr["sd"])
            qlora_mm(2 * blk, lambda j: sl(hb, j), consume_mm1)
            sd_evac = P.ctr["sd"]
            P.wait("vector", "sd", sd_evac)
            if dbg and ch == 1 and blk == 2:
                P.wait("gpsimd", "sd", sd_evac)
                toks = [S_dbg.issue(lambda e, j=j: e.dma_start(
                    out=dbg2_o[j * DT:(j + 1) * DT, :],
                    in_=y[:, j * PC:(j + 1) * PC])) for j in range(NDT)]
                S_dbg.wait("vector", toks[-1])
                S_dbg.wait("scalar", toks[-1])
            panel_ln2(lambda j: sl(y, j), lambda j: sl(hb, j),
                      lambda j: sl(y, j), lambda j: sl(x2b, j), EPS_LN,
                      gamma_ap=lambda j, blk=blk: lng_sb[:, 8 * blk + j:8 * blk + j + 1],
                      beta_ap=lambda j, blk=blk: lnb_sb[:, 8 * blk + j:8 * blk + j + 1],
                      act_func=AF.Gelu)

            def consume_mm2(i):
                P.wait("vector", "td", P.ctr["td"])
                P.op("vector", lambda e, i=i: e.scalar_tensor_tensor(
                    h[:, i * PC:(i + 1) * PC], ps_y[:, 0:PC], 1.0,
                    h[:, i * PC:(i + 1) * PC], ALU.mult, ALU.add), inc=("vd", 1))
                P.wait("tensor", "vd", P.ctr["vd"])
            if dbg and ch == 1 and blk == 2:
                P.wait("gpsimd", "sd", P.ctr["sd"])
                P.wait("gpsimd", "vd", P.ctr["vd"])
                tk4 = [S_dbg.issue(lambda e, j=j: e.dma_start(
                    out=dbg4_o[j * DT:(j + 1) * DT, :],
                    in_=x2b[:, j * PC:(j + 1) * PC])) for j in range(NDT)]
                S_dbg.wait("vector", tk4[-1])
                S_dbg.wait("tensor", tk4[-1])
            qlora_mm(2 * blk + 1, lambda j: sl(x2b, j), consume_mm2)
            if dbg:
                P.wait("gpsimd", "vd", P.ctr["vd"])
                tk = [S_dbg.issue(lambda e, j=j, ch=ch, blk=blk: e.dma_start(
                    out=dbg3_o[ch, blk, j * DT:(j + 1) * DT, :],
                    in_=h[:, j * PC:(j + 1) * PC])) for j in range(NDT)]
                S_dbg.wait("vector", tk[-1])
            if blk < 2:
                P.wait("vector", "vd", P.ctr["vd"])
                for j in range(NDT):
                    P.op("vector", lambda e, j=j: e.tensor_copy(
                        sl(hb, j), sl(h, j)), inc=("vd", 1))
                P.wait("tensor", "vd", P.ctr["vd"])

        P.wait("vector", "vd", P.ctr["vd"])
        for j in range(NDT):
            P.op("vector", lambda e, j=j: e.scalar_tensor_tensor(
                sl(xt, j), sl(h, j), 0.5, sl(xt, j), ALU.mult, ALU.add),
                inc=("vd", 1))
        vd_x1 = P.ctr["vd"]
        P.wait("gpsimd", "vd", vd_x1)
        tox1 = [S_ox.issue(lambda e, j=j, tok0=tok0: e.dma_start(
            out=x1T_o[j * DT:(j + 1) * DT, tok0:tok0 + PC],
            in_=xt[:, j * PC:(j + 1) * PC])) for j in range(NDT)]
        panel_ln2(lambda j: sl(xt, j), lambda j: sl(x2b, j),
                  lambda j: sl(y, j),
                  lambda j, tok0=tok0: xhb[:, j * CT + tok0: j * CT + tok0 + PC], EPS_LN)

    P.wait("tensor", "vd", P.ctr["vd"])
    P.wait("tensor", "sd", P.ctr["sd"])
    xhb_sl = lambda j, n: xhb[:, j * CT + n * 512: j * CT + (n + 1) * 512]
    NQC = CT // 512
    P.wait("sync", "td", P.ctr["td"])
    td_qk = []
    nqk = 0
    for (wap, out_ap, scale) in ((WqT_d, Q_o, 2.0), (WkT_d, K_o, 1.0)):
        for i in range(NDT):
            buf = nqk % 2
            if nqk >= 2:
                P.wait("sync", "td", td_qk[nqk - 2])
            tw = S_w.issue(lambda e, i=i, buf=buf, wap=wap: e.dma_start(
                out=wmm[:, buf * D:(buf + 1) * D].rearrange(
                    "p (j o) -> p j o", j=NDT),
                in_=wap[:, i * DT:(i + 1) * DT].rearrange(
                    "(j p) o -> p j o", p=DT)), parity=buf)
            S_w.wait("tensor", tw)
            for n in range(NQC):
                pssl = ps_y[:, (n % 2) * 512:(n % 2 + 1) * 512]
                for j in range(NDT):
                    P.op("tensor", lambda e, i=i, n=n, j=j, buf=buf, pssl=pssl: e.matmul(
                        pssl, wmm[:, buf * D + j * DT: buf * D + (j + 1) * DT],
                        xhb_sl(j, n), start=(j == 0), stop=(j == NDT - 1)),
                        inc=("td", 1))
                td_mm = P.ctr["td"]
                P.wait("scalar", "td", td_mm)
                P.op("scalar", lambda e, i=i, n=n, pssl=pssl, scale=scale: e.activation(
                    qkev[:, n * 512:(n + 1) * 512],
                    pssl, AF.Copy, scale=scale), inc=("sd", 1))
                if n >= 1:
                    P.wait("tensor", "sd", P.ctr["sd"] - 1)
            P.wait("tensor", "sd", P.ctr["sd"])
            sd_ev = P.ctr["sd"]
            td_qk.append(P.ctr["td"])
            nqk += 1
            P.wait("gpsimd", "sd", sd_ev)
            tq = S_oq.issue(lambda e, i=i, out_ap=out_ap: e.dma_start(
                out=out_ap[i * DT:(i + 1) * DT, :],
                in_=qkev[:, 0:CT]))
            S_oq.wait("scalar", tq)

    tm = S_m.issue(lambda e: e.dma_start(out=wq_f[0:16, 0:CT], in_=kidx16_d))
    S_m.wait("vector", tm)
    P.op("vector", lambda e: e.tensor_scalar(
        x2b[0:16, 0:CT], wq_f[0:16, 0:CT], slopes_sb[:, 0:1], None, ALU.mult),
        inc=("vd", 1))
    vd_aug_a = P.ctr["vd"]
    P.wait("gpsimd", "vd", vd_aug_a)
    ta = S_oa.issue(lambda e: e.dma_start(out=augK_o[0:16, :], in_=x2b[0:16, 0:CT]))
    P.wait("sync", "vd", vd_aug_a)
    tm = S_m.issue(lambda e: e.dma_start(out=wq_f[0:16, 0:CT], in_=ones16_d))
    S_m.wait("vector", tm)
    S_oa.wait("vector", ta)
    P.op("vector", lambda e: e.tensor_scalar(
        x2b[0:16, 0:CT], wq_f[0:16, 0:CT], slopes_sb[:, 0:1], None, ALU.mult),
        inc=("vd", 1))
    vd_aug_b = P.ctr["vd"]
    P.wait("gpsimd", "vd", vd_aug_b)
    ta2 = S_oa.issue(lambda e: e.dma_start(out=augK_o[16:32, :], in_=x2b[0:16, 0:CT]))

    wv_sb = wq_f[:, :].bitcast(BF16)
    P.wait("sync", "vd", P.ctr["vd"])
    S_oq.wait("vector", tq)
    tvv = []
    for nh in range(2):
        P.wait("sync", "td", P.ctr["td"])
        tv = S_v.issue(lambda e, nh=nh: e.dma_start(
            out=wv_sb[:, 0:NDT * 512].rearrange("p (j o) -> p j o", j=NDT),
            in_=WvT_d[:, nh * 512:(nh + 1) * 512].rearrange(
                "(j p) o -> p j o", p=DT)))
        S_v.wait("tensor", tv)
        for i in range(CT // DT):
            pssl = ps_y[:, (i % 2) * 512:(i % 2 + 1) * 512]
            for j in range(NDT):
                P.op("tensor", lambda e, nh=nh, i=i, j=j, pssl=pssl: e.matmul(
                    pssl, xhb[:, j * CT + i * DT: j * CT + (i + 1) * DT],
                    wv_sb[:, j * 512:(j + 1) * 512],
                    start=(j == 0), stop=(j == NDT - 1)), inc=("td", 1))
            td_v = P.ctr["td"]
            P.wait("vector", "td", td_v)
            if len(tvv) >= 2:
                S_ov.wait("vector", tvv[-2])
            P.op("vector", lambda e, i=i, pssl=pssl: e.tensor_copy(
                qkev[:, (i % 2) * 512:(i % 2) * 512 + 512], pssl), inc=("vd", 1))
            vd_v = P.ctr["vd"]
            P.wait("gpsimd", "vd", vd_v)
            tvv.append(S_ov.issue(lambda e, nh=nh, i=i: e.dma_start(
                out=V_o[i * DT:(i + 1) * DT, nh * 512:(nh + 1) * 512],
                in_=qkev[:, (i % 2) * 512:(i % 2) * 512 + 512]), parity=i % 2))
            P.wait("tensor", "vd", vd_v)

    S_ox.wait("sync", tox1[-1])
    S_oq.wait("sync", tq)
    S_oa.wait("sync", ta2)
    S_ov.wait("sync", tvv[-1])
    S_ov.wait("sync", tvv[-2])
    P.wait("sync", "gd", P.ctr["gd"])
    return P



def build_launch_C():
    P = Prog("C")
    nc = P.nc

    x1T_d = nc.dram_tensor("x1T", [D, CT], F32, kind="ExternalInput").ap()
    OT_d = nc.dram_tensor("OT", [D, CT], BF16, kind="ExternalInput").ap()
    WoT_d = nc.dram_tensor("WoT", [D, D], BF16, kind="ExternalInput").ap()
    wdeq_d = nc.dram_tensor("wdeq", [6, D, D], BF16, kind="ExternalInput").ap()
    fbias_d = nc.dram_tensor("fbias", [6, D], BF16, kind="ExternalInput").ap()
    laT_d = nc.dram_tensor("laT", [6, D, 32], BF16, kind="ExternalInput").ap()
    lbT_d = nc.dram_tensor("lbT", [6, 32, D], BF16, kind="ExternalInput").ap()
    lng_d = nc.dram_tensor("ln_g_cols", [DT, 3 * NDT], F32, kind="ExternalInput").ap()
    lnb_d = nc.dram_tensor("ln_b_cols", [DT, 3 * NDT], F32, kind="ExternalInput").ap()

    x3T_o = nc.dram_tensor("x3T", [D, CT], F32, kind="ExternalOutput").ap()
    xh3T_o = nc.dram_tensor("xh3T", [D, CT], BF16, kind="ExternalOutput").ap()
    xh3fT_o = nc.dram_tensor("xh3fT", [D, CT], F32, kind="ExternalOutput").ap()
    part_o = nc.dram_tensor("part", [D, 1], F32, kind="ExternalOutput").ap()

    P.register_consts([0.5, 2.0, -1.0, 1.0 / 1024.0, EPS_LN, -0.5])

    xt = P.sb("xt", [DT, NDT * PC])
    h = P.sb("h", [DT, NDT * PC])
    hb = P.sb("hb", [DT, NDT * PC], BF16)
    y = P.sb("y", [DT, NDT * PC])
    x2b = P.sb("x2b", [DT, NDT * PC], BF16)
    ob = P.sb("ob", [DT, NDT * PC], BF16)
    sqs = P.sb("sqs", [DT, 2 * 512], BF16)
    wmm = P.sb("wmm", [DT, 2 * D], BF16)
    la_sb = P.sb("la_sb", [DT, 6 * NDT * 32], BF16)
    lb_sb = P.sb("lb_sb", [32, D], BF16)
    bias_sb = P.sb("bias_sb", [1, D], BF16)
    lng_sb = P.sb("lng_sb", [DT, 3 * NDT])
    lnb_sb = P.sb("lnb_sb", [DT, 3 * NDT])
    lorar = P.sb("lorar", [32, PC], BF16)
    rows = P.sb("rows", [1, 2 * 512])
    rowb = P.sb("rowb", [1, 2 * PC], BF16)
    ones_col = P.sb("ones_col", [DT, 1], BF16)
    ones_row = P.sb("ones_row", [1, PC], BF16)
    pred = P.sb("pred", [DT, NDT])
    ptmp = P.sb("ptmp", [DT, 1])

    ps_y = P.psum("ps_y", [DT, PC])
    ps_bc = P.psum("ps_bc", [DT, PC])
    ps_row = P.psum("ps_row", [33, 512])
    ps_r = P.psum("ps_r", [32, PC])

    for s in ["vd", "sd", "td", "cd"]:
        P.sem(s)
    S_param = DmaStream(P, "lp", "sync", 1)
    S_w = DmaStream(P, "lw", "sync", 2)
    S_x = DmaStream(P, "lx", "sync", 1)
    S_ob = DmaStream(P, "lo", "sync", 1)
    S_lb = DmaStream(P, "llb", "sync", 1)
    S_ox = DmaStream(P, "gx", "gpsimd", 1)
    S_oh = DmaStream(P, "gh", "gpsimd", 1)
    S_ohf = DmaStream(P, "gf", "gpsimd", 1)
    S_op = DmaStream(P, "gp", "gpsimd", 1)

    P.emit_const_memsets("gpsimd", ("cd", 1))
    cd_init = P.ctr["cd"]

    for j in range(NDT):
        S_param.issue(lambda e, j=j: e.dma_start(
            out=la_sb[:, :].rearrange("p (w j r) -> p w j r", w=6, j=NDT)[:, :, j, :],
            in_=laT_d[:, j * DT:(j + 1) * DT, :].rearrange("w p r -> p w r")))
    S_param.issue(lambda e: e.dma_start(out=lng_sb[:, :], in_=lng_d))
    tok_params = S_param.issue(lambda e: e.dma_start(out=lnb_sb[:, :], in_=lnb_d))
    P.wait("vector", "cd", cd_init)
    P.op("vector", lambda e: e.memset(ones_col[:, :], 1.0), inc=("vd", 1))
    P.op("vector", lambda e: e.memset(ones_row[:, :], 1.0), inc=("vd", 1))
    P.op("vector", lambda e: e.memset(pred[:, :], 0.0), inc=("vd", 1))
    vd_ones = P.ctr["vd"]

    ctx = dict(P=P, sqs=sqs, rows=rows, rowb=rowb, ones_col=ones_col,
               ones_row=ones_row, ps_row=ps_row, ps_bc=ps_bc,
               ps_r=ps_r, ps_y=ps_y, lorar=lorar, wmm=wmm, la_sb=la_sb,
               lb_sb=lb_sb, bias_sb=bias_sb, S_w=S_w, S_lb=S_lb,
               wdeq_ap=wdeq_d, lbT_ap=lbT_d, fbias_ap=fbias_d)

    tox3 = []
    for ch in range(2):
        tok0 = ch * PC
        if ch > 0:
            P.wait("sync", "vd", P.ctr["vd"])
            P.wait("sync", "sd", P.ctr["sd"])
            S_ox.wait("sync", tox3[-1])
        tox = [S_x.issue(lambda e, j=j, tok0=tok0: e.dma_start(
            out=xt[:, j * PC:(j + 1) * PC],
            in_=x1T_d[j * DT:(j + 1) * DT, tok0:tok0 + PC])) for j in range(NDT)]
        tob = [S_ob.issue(lambda e, j=j, tok0=tok0: e.dma_start(
            out=ob[:, j * PC:(j + 1) * PC],
            in_=OT_d[j * DT:(j + 1) * DT, tok0:tok0 + PC])) for j in range(NDT)]
        for eng in ("vector", "scalar", "tensor"):
            S_x.wait(eng, tox[-1])
            S_ob.wait(eng, tob[-1])
            S_param.wait(eng, tok_params)
        P.wait("vector", "vd", vd_ones)

        sl = lambda buf, j: buf[:, j * PC:(j + 1) * PC]

        td_w = []
        for i in range(NDT):
            buf = i % 2
            if i >= 2:
                P.wait("sync", "td", td_w[i - 2])
            tw = S_w.issue(lambda e, i=i, buf=buf: e.dma_start(
                out=wmm[:, buf * D:(buf + 1) * D].rearrange(
                    "p (j o) -> p j o", j=NDT),
                in_=WoT_d[:, i * DT:(i + 1) * DT].rearrange(
                    "(j p) o -> p j o", p=DT)), parity=buf)
            S_w.wait("tensor", tw)
            for c in range(NSC):
                for j in range(NDT):
                    P.op("tensor", lambda e, i=i, c=c, j=j, buf=buf: e.matmul(
                        ps_y[:, c * 512:(c + 1) * 512],
                        wmm[:, buf * D + j * DT: buf * D + (j + 1) * DT],
                        ob[:, j * PC + c * 512: j * PC + (c + 1) * 512],
                        start=(j == 0), stop=(j == NDT - 1)), inc=("td", 1))
            td_w.append(P.ctr["td"])
            P.wait("vector", "td", P.ctr["td"])
            P.op("vector", lambda e, i=i: e.scalar_tensor_tensor(
                xt[:, i * PC:(i + 1) * PC], ps_y[:, 0:PC], 1.0,
                xt[:, i * PC:(i + 1) * PC], ALU.mult, ALU.add), inc=("vd", 1))
            P.wait("tensor", "vd", P.ctr["vd"])

        emit_panel_ln(ctx, lambda j: sl(xt, j), lambda j: sl(x2b, j),
                      lambda j: sl(h, j), lambda j: sl(hb, j), EPS_LN)
        for blk in range(3):
            def consume_mm1(i):
                P.wait("scalar", "td", P.ctr["td"])
                P.op("scalar", lambda e, i=i: e.activation(
                    y[:, i * PC:(i + 1) * PC], ps_y[:, 0:PC], AF.Copy),
                    inc=("sd", 1))
                P.wait("tensor", "sd", P.ctr["sd"])
            emit_qlora_mm(ctx, 2 * blk, lambda j: sl(hb, j), consume_mm1)
            P.wait("vector", "sd", P.ctr["sd"])
            emit_panel_ln(ctx, lambda j: sl(y, j), lambda j: sl(hb, j),
                          lambda j: sl(y, j), lambda j: sl(x2b, j), EPS_LN,
                          gamma_ap=lambda j, blk=blk: lng_sb[:, 8 * blk + j:8 * blk + j + 1],
                          beta_ap=lambda j, blk=blk: lnb_sb[:, 8 * blk + j:8 * blk + j + 1],
                          act_func=AF.Gelu)

            def consume_mm2(i):
                P.wait("vector", "td", P.ctr["td"])
                P.op("vector", lambda e, i=i: e.scalar_tensor_tensor(
                    h[:, i * PC:(i + 1) * PC], ps_y[:, 0:PC], 1.0,
                    h[:, i * PC:(i + 1) * PC], ALU.mult, ALU.add), inc=("vd", 1))
                P.wait("tensor", "vd", P.ctr["vd"])
            emit_qlora_mm(ctx, 2 * blk + 1, lambda j: sl(x2b, j), consume_mm2)
            if blk < 2:
                P.wait("vector", "vd", P.ctr["vd"])
                for j in range(NDT):
                    P.op("vector", lambda e, j=j: e.tensor_copy(
                        sl(hb, j), sl(h, j)), inc=("vd", 1))
                P.wait("tensor", "vd", P.ctr["vd"])

        P.wait("vector", "vd", P.ctr["vd"])
        for j in range(NDT):
            P.op("vector", lambda e, j=j: e.scalar_tensor_tensor(
                sl(xt, j), sl(h, j), 0.5, sl(xt, j), ALU.mult, ALU.add),
                inc=("vd", 1))
        vd_x3 = P.ctr["vd"]
        P.wait("gpsimd", "vd", vd_x3)
        tox3 = [S_ox.issue(lambda e, j=j, tok0=tok0: e.dma_start(
            out=x3T_o[j * DT:(j + 1) * DT, tok0:tok0 + PC],
            in_=xt[:, j * PC:(j + 1) * PC])) for j in range(NDT)]
        emit_panel_ln(ctx, lambda j: sl(xt, j), lambda j: sl(hb, j),
                      lambda j: sl(y, j), lambda j: sl(x2b, j), EPS_LN)
        vd_ln = P.ctr["vd"]
        P.wait("gpsimd", "vd", vd_ln)
        P.wait("gpsimd", "sd", P.ctr["sd"])
        toh = [S_oh.issue(lambda e, j=j, tok0=tok0: e.dma_start(
            out=xh3T_o[j * DT:(j + 1) * DT, tok0:tok0 + PC],
            in_=x2b[:, j * PC:(j + 1) * PC])) for j in range(NDT)]
        tohf = [S_ohf.issue(lambda e, j=j, tok0=tok0: e.dma_start(
            out=xh3fT_o[j * DT:(j + 1) * DT, tok0:tok0 + PC],
            in_=y[:, j * PC:(j + 1) * PC])) for j in range(NDT)]
        for j in range(NDT):
            P.op("vector", lambda e, j=j: e.reduce_sum(
                ptmp[:, 0:1], y[:, j * PC:(j + 1) * PC],
                axis=mybir.AxisListType.X), inc=("vd", 1))
            P.op("vector", lambda e, j=j: e.scalar_tensor_tensor(
                pred[:, j:j + 1], ptmp[:, 0:1], 1.0, pred[:, j:j + 1],
                ALU.mult, ALU.add), inc=("vd", 1))
        S_oh.wait("vector", toh[-1])
        S_ohf.wait("vector", tohf[-1])
        S_oh.wait("scalar", toh[-1])
        S_ohf.wait("scalar", tohf[-1])

    P.wait("gpsimd", "vd", P.ctr["vd"])
    top = [S_op.issue(lambda e, j=j: e.dma_start(
        out=part_o[j * DT:(j + 1) * DT, :], in_=pred[:, j:j + 1]))
        for j in range(NDT)]
    S_ox.wait("sync", tox3[-1])
    S_oh.wait("sync", toh[-1])
    S_ohf.wait("sync", tohf[-1])
    S_op.wait("sync", top[-1])
    return P



def build_launch_Fw():
    P = Prog("Fw")
    nc = P.nc

    x4T_d = nc.dram_tensor("x4T", [D, CT], F32, kind="ExternalInput").ap()
    xh4T_d = nc.dram_tensor("xh4fT", [D, CT], F32, kind="ExternalInput").ap()
    yg_d = nc.dram_tensor("ygTf", [2048, CT], BF16, kind="ExternalInput").ap()
    ssq_d = nc.dram_tensor("ssq4", [4, CT], F32, kind="ExternalInput").ap()
    nwc_d = nc.dram_tensor("norm_w_c", [DT, 16], F32, kind="ExternalInput").ap()
    wo_d = nc.dram_tensor("WoutT", [2048, D], BF16, kind="ExternalInput").ap()

    x5T_o = nc.dram_tensor("x5T", [D, CT], F32, kind="ExternalOutput").ap()
    xh5fT_o = nc.dram_tensor("xh5fT", [D, CT], F32, kind="ExternalOutput").ap()
    part_o = nc.dram_tensor("part", [D, 1], F32, kind="ExternalOutput").ap()

    P.register_consts([1.0 / 2048.0, EPS_LN, -0.5, 1.0, 1.0 / 1024.0])

    xt = P.sb("xt", [DT, NDT * PC])
    xh = P.sb("xh", [DT, NDT * PC])
    ybuf = P.sb("ybuf", [DT, 16 * PC], BF16)
    wmm = P.sb("wmm", [DT, 2 * 2048], BF16)
    nwc = P.sb("nwc", [DT, 16])
    ssb = P.sb("ssb", [1, 4 * PC])
    rsum = P.sb("rsum", [1, PC])
    rr = P.sb("rr", [1, PC])
    rrb = P.sb("rrb", [1, 2 * PC], BF16)
    rtmp = P.sb("rtmp", [1, PC])
    sqs = P.sb("sqs", [DT, 2 * 512], BF16)
    rows = P.sb("rows", [1, 2 * 512])
    rowb = P.sb("rowb", [1, 2 * PC], BF16)
    ones_col = P.sb("ones_col", [DT, 1], BF16)
    ones_row = P.sb("ones_row", [1, PC], BF16)
    pred = P.sb("pred", [DT, NDT])
    ptmp = P.sb("ptmp", [DT, 1])

    ps_y = P.psum("ps_y", [DT, PC])
    ps_bc = P.psum("ps_bc", [DT, PC])
    ps_row = P.psum("ps_row", [33, 512])

    for s in ["vd", "sd", "td", "cd"]:
        P.sem(s)
    S_param = DmaStream(P, "lp", "sync", 1)
    S_w = DmaStream(P, "lw", "sync", 2)
    S_x = DmaStream(P, "lx", "sync", 1)
    S_yb = DmaStream(P, "ly", "sync", 1)
    S_ox = DmaStream(P, "gx", "gpsimd", 1)
    S_oh = DmaStream(P, "gh", "gpsimd", 1)
    S_op = DmaStream(P, "gp", "gpsimd", 1)

    P.emit_const_memsets("gpsimd", ("cd", 1))
    cd_init = P.ctr["cd"]

    tok_params = S_param.issue(lambda e: e.dma_start(out=nwc[:, :], in_=nwc_d))
    P.wait("vector", "cd", cd_init)
    P.wait("scalar", "cd", cd_init)
    P.op("vector", lambda e: e.memset(ones_col[:, :], 1.0), inc=("vd", 1))
    P.op("vector", lambda e: e.memset(ones_row[:, :], 1.0), inc=("vd", 1))
    P.op("vector", lambda e: e.memset(pred[:, :], 0.0), inc=("vd", 1))
    vd_ones = P.ctr["vd"]

    ctx = dict(P=P, sqs=sqs, rows=rows, rowb=rowb, ones_col=ones_col,
               ones_row=ones_row, ps_row=ps_row, ps_bc=ps_bc)

    toh = []
    tox5 = []
    for ch in range(2):
        tok0 = ch * PC
        if ch > 0:
            P.wait("sync", "vd", P.ctr["vd"])
            P.wait("sync", "sd", P.ctr["sd"])
            S_ox.wait("sync", tox5[-1])
            S_oh.wait("sync", toh[-1])
        tox = [S_x.issue(lambda e, j=j, tok0=tok0: e.dma_start(
            out=xt[:, j * PC:(j + 1) * PC],
            in_=x4T_d[j * DT:(j + 1) * DT, tok0:tok0 + PC])) for j in range(NDT)]
        tox += [S_x.issue(lambda e, j=j, tok0=tok0: e.dma_start(
            out=xh[:, j * PC:(j + 1) * PC],
            in_=xh4T_d[j * DT:(j + 1) * DT, tok0:tok0 + PC])) for j in range(NDT)]
        toy = [S_yb.issue(lambda e, k=k, tok0=tok0: e.dma_start(
            out=ybuf[:, k * PC:(k + 1) * PC],
            in_=yg_d[k * DT:(k + 1) * DT, tok0:tok0 + PC])) for k in range(16)]
        toy += [S_yb.issue(lambda e, g=g, tok0=tok0: e.dma_start(
            out=ssb[0:1, g * PC:(g + 1) * PC],
            in_=ssq_d[g:g + 1, tok0:tok0 + PC])) for g in range(4)]
        for eng in ("vector", "scalar", "tensor"):
            S_x.wait(eng, tox[-1])
            S_yb.wait(eng, toy[-1])
            S_param.wait(eng, tok_params)
        P.wait("vector", "vd", vd_ones)

        P.op("vector", lambda e: e.tensor_tensor(
            rsum[:, :], ssb[0:1, 0:PC], ssb[0:1, PC:2 * PC], ALU.add),
            inc=("vd", 1))
        P.op("vector", lambda e: e.tensor_tensor(
            rsum[:, :], rsum[:, :], ssb[0:1, 2 * PC:3 * PC], ALU.add),
            inc=("vd", 1))
        P.op("vector", lambda e: e.tensor_tensor(
            rsum[:, :], rsum[:, :], ssb[0:1, 3 * PC:4 * PC], ALU.add),
            inc=("vd", 1))
        P.wait("scalar", "vd", P.ctr["vd"])
        P.op("scalar", lambda e: e.activation(
            rr[:, :], rsum[:, :], AF.Ln, scale=1.0 / 2048.0, bias=EPS_LN),
            inc=("sd", 1))
        P.op("scalar", lambda e: e.activation(
            rr[:, :], rr[:, :], AF.Exp, scale=-0.5), inc=("sd", 1))
        P.wait("vector", "sd", P.ctr["sd"])
        P.op("vector", lambda e: e.tensor_copy(rrb[:, 0:PC], rr[:, :]),
             inc=("vd", 1))
        P.op("vector", lambda e: e.tensor_tensor(
            rtmp[:, :], rr[:, :], rrb[:, 0:PC], ALU.subtract), inc=("vd", 1))
        P.op("vector", lambda e: e.tensor_copy(rrb[:, PC:2 * PC], rtmp[:, :]),
             inc=("vd", 1))
        P.wait("tensor", "vd", P.ctr["vd"])
        for c in range(NSC):
            P.op("tensor", lambda e, c=c: e.matmul(
                ps_bc[:, c * 512:(c + 1) * 512], ones_row[0:1, 0:DT],
                rrb[0:1, c * 512:(c + 1) * 512], start=True, stop=False),
                inc=("td", 1))
            P.op("tensor", lambda e, c=c: e.matmul(
                ps_bc[:, c * 512:(c + 1) * 512], ones_row[0:1, 0:DT],
                rrb[0:1, PC + c * 512: PC + (c + 1) * 512], start=False,
                stop=True), inc=("td", 1))
        P.wait("vector", "td", P.ctr["td"])
        for k in range(16):
            for c in range(NSC):
                P.op("vector", lambda e, k=k, c=c: e.tensor_tensor(
                    ybuf[:, k * PC + c * 512: k * PC + (c + 1) * 512],
                    ybuf[:, k * PC + c * 512: k * PC + (c + 1) * 512],
                    ps_bc[:, c * 512:(c + 1) * 512], ALU.mult), inc=("vd", 1))
        for j in range(NDT):
            P.op("vector", lambda e, j=j: e.tensor_tensor(
                xt[:, j * PC:(j + 1) * PC], xt[:, j * PC:(j + 1) * PC],
                xh[:, j * PC:(j + 1) * PC], ALU.add), inc=("vd", 1))
        vd_y = P.ctr["vd"]
        P.wait("tensor", "vd", vd_y)

        td_w = []
        for i in range(NDT):
            buf = i % 2
            if i >= 2:
                P.wait("sync", "td", td_w[i - 2])
            tw = S_w.issue(lambda e, i=i, buf=buf: e.dma_start(
                out=wmm[:, buf * 2048:(buf + 1) * 2048].rearrange(
                    "p (k o) -> p k o", k=16),
                in_=wo_d[:, i * DT:(i + 1) * DT].rearrange(
                    "(k p) o -> p k o", p=DT)), parity=buf)
            S_w.wait("vector", tw)
            for k in range(16):
                P.op("vector", lambda e, i=i, k=k, buf=buf: e.tensor_scalar(
                    wmm[:, buf * 2048 + k * DT: buf * 2048 + (k + 1) * DT],
                    wmm[:, buf * 2048 + k * DT: buf * 2048 + (k + 1) * DT],
                    nwc[:, k:k + 1], None, ALU.mult), inc=("vd", 1))
            vd_w = P.ctr["vd"]
            P.wait("tensor", "vd", vd_w)
            for c in range(NSC):
                for k in range(16):
                    P.op("tensor", lambda e, i=i, c=c, k=k, buf=buf: e.matmul(
                        ps_y[:, c * 512:(c + 1) * 512],
                        wmm[:, buf * 2048 + k * DT: buf * 2048 + (k + 1) * DT],
                        ybuf[:, k * PC + c * 512: k * PC + (c + 1) * 512],
                        start=(k == 0), stop=(k == 15)), inc=("td", 1))
            td_w.append(P.ctr["td"])
            P.wait("vector", "td", P.ctr["td"])
            P.op("vector", lambda e, i=i: e.tensor_tensor(
                xh[:, i * PC:(i + 1) * PC], ps_y[:, 0:PC],
                xt[:, i * PC:(i + 1) * PC], ALU.add), inc=("vd", 1))
            P.wait("tensor", "vd", P.ctr["vd"])
        vd_x5 = P.ctr["vd"]
        P.wait("gpsimd", "vd", vd_x5)
        tox5 = [S_ox.issue(lambda e, j=j, tok0=tok0: e.dma_start(
            out=x5T_o[j * DT:(j + 1) * DT, tok0:tok0 + PC],
            in_=xh[:, j * PC:(j + 1) * PC])) for j in range(NDT)]
        sl = lambda buf, j: buf[:, j * PC:(j + 1) * PC]
        emit_panel_ln(ctx, lambda j: sl(xh, j),
                      lambda j: ybuf[:, j * PC:(j + 1) * PC],
                      lambda j: sl(xt, j),
                      lambda j: ybuf[:, (8 + j) * PC:(9 + j) * PC], EPS_LN)
        vd_ln = P.ctr["vd"]
        P.wait("gpsimd", "vd", vd_ln)
        toh = [S_oh.issue(lambda e, j=j, tok0=tok0: e.dma_start(
            out=xh5fT_o[j * DT:(j + 1) * DT, tok0:tok0 + PC],
            in_=xt[:, j * PC:(j + 1) * PC])) for j in range(NDT)]
        for j in range(NDT):
            P.op("vector", lambda e, j=j: e.reduce_sum(
                ptmp[:, 0:1], xt[:, j * PC:(j + 1) * PC],
                axis=mybir.AxisListType.X), inc=("vd", 1))
            P.op("vector", lambda e, j=j: e.scalar_tensor_tensor(
                pred[:, j:j + 1], ptmp[:, 0:1], 1.0, pred[:, j:j + 1],
                ALU.mult, ALU.add), inc=("vd", 1))
        S_oh.wait("vector", toh[-1])

    P.wait("gpsimd", "vd", P.ctr["vd"])
    top = [S_op.issue(lambda e, j=j: e.dma_start(
        out=part_o[j * DT:(j + 1) * DT, :], in_=pred[:, j:j + 1]))
        for j in range(NDT)]
    S_ox.wait("sync", tox5[-1])
    S_oh.wait("sync", toh[-1])
    S_op.wait("sync", top[-1])
    return P



TWIST_SRC = [4, 5, 6, 7, 0, 1, 2, 3]
TWIST_SGN = [1.0, 1.0, -1.0, -1.0, -1.0, -1.0, 1.0, 1.0]


def build_launch_L1FFN():
    P = Prog("L1FFN")
    nc = P.nc

    xT_d = nc.dram_tensor("xT", [D, CT], F32, kind="ExternalInput").ap()
    xhfT_d = nc.dram_tensor("xhfT", [D, CT], F32, kind="ExternalInput").ap()
    parts4_d = nc.dram_tensor("parts4", [4, D], F32, kind="ExternalInput").ap()
    cw1_d = nc.dram_tensor("cn_w1T", [D, D], BF16, kind="ExternalInput").ap()
    cw2_d = nc.dram_tensor("cn_w2T", [D, 512], BF16, kind="ExternalInput").ap()
    cw3_d = nc.dram_tensor("cn_w3T", [512, 256], BF16, kind="ExternalInput").ap()
    cw4_d = nc.dram_tensor("cn_w4c", [DT, 2], BF16, kind="ExternalInput").ap()
    gw1_d = nc.dram_tensor("gc_w1T", [D, 512], BF16, kind="ExternalInput").ap()
    gw2_d = nc.dram_tensor("gc_w2T", [512, 256], BF16, kind="ExternalInput").ap()
    gw3_d = nc.dram_tensor("gc_w3c", [DT, 2], BF16, kind="ExternalInput").ap()
    cb1_d = nc.dram_tensor("cn_b1c", [DT, NDT], F32, kind="ExternalInput").ap()
    cb2_d = nc.dram_tensor("cn_b2c", [DT, 4], F32, kind="ExternalInput").ap()
    cb3_d = nc.dram_tensor("cn_b3c", [DT, 2], F32, kind="ExternalInput").ap()
    cb4_d = nc.dram_tensor("cn_b4c", [1, 1], F32, kind="ExternalInput").ap()
    gb1_d = nc.dram_tensor("gc_b1c", [DT, 4], F32, kind="ExternalInput").ap()
    gb2_d = nc.dram_tensor("gc_b2c", [DT, 2], F32, kind="ExternalInput").ap()
    gb3_d = nc.dram_tensor("gc_b3c", [1, 1], F32, kind="ExternalInput").ap()
    ar_d = nc.dram_tensor("ar", [1, 1], F32, kind="ExternalInput").ap()

    xoutT_o = nc.dram_tensor("xoutT", [D, CT], F32, kind="ExternalOutput").ap()
    g_scr = nc.dram_tensor("g_scr", [D], F32, kind="Internal").ap()

    P.register_consts([0.5, 1.0 / T, 0.6, 1.4, 0.1, -1.0])

    xt = P.sb("xt", [DT, NDT * PC])
    outA = P.sb("outA", [DT, NDT * PC])
    outB = P.sb("outB", [DT, NDT * PC])
    outb = P.sb("outb", [DT, NDT * PC], BF16)
    c1 = P.sb("c1", [DT, NDT * PC], BF16)
    c2 = P.sb("c2", [DT, 4 * PC], BF16)
    c3 = P.sb("c3", [DT, 2 * PC], BF16)
    wmm = P.sb("wmm", [DT, 2 * D], BF16)
    w4_sb = P.sb("w4_sb", [DT, 2], BF16)
    bias_cols = P.sb("bias_cols", [DT, NDT + 4 + 2 + 4 + 2])
    b4g3_sb = P.sb("b4g3_sb", [1, 2])
    parts_sb = P.sb("parts_sb", [4, D])
    partsb_sb = P.sb("partsb_sb", [4, D], BF16)
    ones4_sb = P.sb("ones4_sb", [4, 1], BF16)
    grow_sb = P.sb("grow_sb", [1, D])
    gcol_sb = P.sb("gcol_sb", [DT, NDT], BF16)
    g1_sb = P.sb("g1_sb", [DT, 4], BF16)
    g2_sb = P.sb("g2_sb", [DT, 2], BF16)
    scal_sb = P.sb("scal_sb", [1, 8])
    tf_sb = P.sb("tf_sb", [1, PC])
    cpb_sb = P.sb("cpb_sb", [1, PC], BF16)
    tmpc_sb = P.sb("tmpc_sb", [DT, 512])
    ones_row = P.sb("ones_row", [1, PC], BF16)

    ps_y = P.psum("ps_y", [DT, PC])
    ps_bc = P.psum("ps_bc", [DT, 512])
    ps_g = P.psum("ps_g", [DT, 4])

    for s in ["vd", "sd", "td", "cd"]:
        P.sem(s)
    S_param = DmaStream(P, "lp", "sync", 1)
    S_w = DmaStream(P, "lw", "sync", 2)
    S_x = DmaStream(P, "lx", "sync", 1)
    S_g = DmaStream(P, "lg", "sync", 1)
    S_ox = DmaStream(P, "gx", "gpsimd", 1)
    S_gs = DmaStream(P, "gs", "gpsimd", 1)

    P.emit_const_memsets("gpsimd", ("cd", 1))
    cd_init = P.ctr["cd"]

    S_param.issue(lambda e: e.dma_start(out=parts_sb[:, :], in_=parts4_d))
    S_param.issue(lambda e: e.dma_start(out=bias_cols[:, 0:NDT], in_=cb1_d))
    S_param.issue(lambda e: e.dma_start(out=bias_cols[:, NDT:NDT + 4], in_=cb2_d))
    S_param.issue(lambda e: e.dma_start(
        out=bias_cols[:, NDT + 4:NDT + 6], in_=cb3_d))
    S_param.issue(lambda e: e.dma_start(
        out=bias_cols[:, NDT + 6:NDT + 10], in_=gb1_d))
    S_param.issue(lambda e: e.dma_start(
        out=bias_cols[:, NDT + 10:NDT + 12], in_=gb2_d))
    S_param.issue(lambda e: e.dma_start(out=b4g3_sb[0:1, 0:1], in_=cb4_d))
    S_param.issue(lambda e: e.dma_start(out=b4g3_sb[0:1, 1:2], in_=gb3_d))
    S_param.issue(lambda e: e.dma_start(out=scal_sb[0:1, 1:2], in_=ar_d))
    tok_w4 = S_param.issue(lambda e: e.dma_start(
        out=w4_sb[:, :], in_=cw4_d))
    for eng in ("vector", "scalar", "tensor"):
        S_param.wait(eng, tok_w4)
    P.wait("vector", "cd", cd_init)
    P.op("vector", lambda e: e.memset(ones4_sb[:, :], 1.0), inc=("vd", 1))
    P.op("vector", lambda e: e.memset(ones_row[:, :], 1.0), inc=("vd", 1))
    P.op("vector", lambda e: e.tensor_copy(partsb_sb[:, :], parts_sb[:, :]),
         inc=("vd", 1))
    vd_prep = P.ctr["vd"]

    gcolf_sb = P.sb("gcolf_sb", [DT, NDT])
    P.wait("tensor", "vd", vd_prep)
    for c in range(2):
        P.op("tensor", lambda e, c=c: e.matmul(
            ps_y[0:1, c * 512:(c + 1) * 512], ones4_sb[:, :],
            partsb_sb[:, c * 512:(c + 1) * 512], start=True, stop=True),
            inc=("td", 1))
    P.wait("vector", "td", P.ctr["td"])
    P.op("vector", lambda e: e.tensor_scalar(
        grow_sb[:, :], ps_y[0:1, 0:D], 1.0 / T, None, ALU.mult), inc=("vd", 1))
    P.wait("gpsimd", "vd", P.ctr["vd"])
    tgs = S_gs.issue(lambda e: e.dma_start(out=g_scr, in_=grow_sb[0:1, :]))
    S_gs.wait("sync", tgs)
    for j in range(NDT):
        tgl = S_g.issue(lambda e, j=j: e.dma_start(
            out=gcolf_sb[:, j:j + 1],
            in_=g_scr[j * DT:(j + 1) * DT].rearrange("(p o) -> p o", o=1)))
    S_g.wait("vector", tgl)
    P.op("vector", lambda e: e.tensor_copy(gcol_sb[:, :], gcolf_sb[:, :]),
         inc=("vd", 1))
    vd_gcol = P.ctr["vd"]
    P.wait("tensor", "vd", vd_gcol)
    for i in range(4):
        buf = i % 2
        if i >= 2:
            P.wait("sync", "td", td_g[i - 2])
        tw = S_w.issue(lambda e, i=i, buf=buf: e.dma_start(
            out=wmm[:, buf * D:(buf + 1) * D].rearrange("p (j o) -> p j o", j=NDT),
            in_=gw1_d[:, i * DT:(i + 1) * DT].rearrange("(j p) o -> p j o", p=DT)),
            parity=buf)
        S_w.wait("tensor", tw)
        for j in range(NDT):
            P.op("tensor", lambda e, i=i, j=j, buf=buf: e.matmul(
                ps_g[:, i:i + 1],
                wmm[:, buf * D + j * DT: buf * D + (j + 1) * DT],
                gcol_sb[:, j:j + 1], start=(j == 0), stop=(j == NDT - 1)),
                inc=("td", 1))
        if i == 0:
            td_g = []
        td_g.append(P.ctr["td"])
    P.wait("scalar", "td", P.ctr["td"])
    P.op("scalar", lambda e: e.activation(
        g1_sb[:, :], ps_g[:, 0:4], AF.Gelu,
        bias=bias_cols[:, NDT + 6:NDT + 7]) if False else e.activation(
        g1_sb[:, 0:1], ps_g[:, 0:1], AF.Gelu, bias=bias_cols[:, NDT + 6:NDT + 7]),
        inc=("sd", 1))
    for i in range(1, 4):
        P.op("scalar", lambda e, i=i: e.activation(
            g1_sb[:, i:i + 1], ps_g[:, i:i + 1], AF.Gelu,
            bias=bias_cols[:, NDT + 6 + i:NDT + 7 + i]), inc=("sd", 1))
    sd_g1 = P.ctr["sd"]
    P.wait("tensor", "sd", sd_g1)
    P.wait("sync", "td", P.ctr["td"])
    tw = S_w.issue(lambda e: e.dma_start(
        out=wmm[:, 0:4 * DT].rearrange("p (j o) -> p j o", j=4),
        in_=gw2_d[:, :].rearrange("(j p) o -> p j o", p=DT)
        [:, :, 0:DT] if False else gw2_d[:, 0:DT].rearrange(
            "(j p) o -> p j o", p=DT)), parity=0)
    tw2 = S_w.issue(lambda e: e.dma_start(
        out=wmm[:, 4 * DT:8 * DT].rearrange("p (j o) -> p j o", j=4),
        in_=gw2_d[:, DT:2 * DT].rearrange("(j p) o -> p j o", p=DT)), parity=1)
    S_w.wait("tensor", tw)
    S_w.wait("tensor", tw2)
    for i in range(2):
        for j in range(4):
            P.op("tensor", lambda e, i=i, j=j: e.matmul(
                ps_g[:, i:i + 1], wmm[:, (4 * i + j) * DT:(4 * i + j + 1) * DT],
                g1_sb[:, j:j + 1], start=(j == 0), stop=(j == 3)), inc=("td", 1))
    P.wait("scalar", "td", P.ctr["td"])
    for i in range(2):
        P.op("scalar", lambda e, i=i: e.activation(
            g2_sb[:, i:i + 1], ps_g[:, i:i + 1], AF.Gelu,
            bias=bias_cols[:, NDT + 10 + i:NDT + 11 + i]), inc=("sd", 1))
    sd_g2 = P.ctr["sd"]
    P.wait("sync", "td", P.ctr["td"])
    tw = S_w.issue(lambda e: e.dma_start(
        out=wmm[:, 0:2], in_=gw3_d), parity=0)
    S_w.wait("tensor", tw)
    P.wait("tensor", "sd", sd_g2)
    for j in range(2):
        P.op("tensor", lambda e, j=j: e.matmul(
            ps_g[0:1, 3:4], wmm[:, j:j + 1], g2_sb[:, j:j + 1],
            start=(j == 0), stop=(j == 1)), inc=("td", 1))
    P.wait("scalar", "td", P.ctr["td"])
    P.op("scalar", lambda e: e.activation(
        scal_sb[0:1, 0:1], ps_g[0:1, 3:4], AF.Sigmoid,
        bias=b4g3_sb[0:1, 1:2]), inc=("sd", 1))
    sd_gf = P.ctr["sd"]
    P.wait("vector", "sd", sd_gf)
    P.op("vector", lambda e: e.tensor_scalar(
        scal_sb[0:1, 2:3], scal_sb[0:1, 1:2], 0.6, None, ALU.mult), inc=("vd", 1))
    P.op("vector", lambda e: e.tensor_tensor(
        scal_sb[0:1, 4:5], scal_sb[0:1, 0:1], scal_sb[0:1, 1:2], ALU.mult),
        inc=("vd", 1))
    P.op("vector", lambda e: e.tensor_scalar(
        scal_sb[0:1, 4:5], scal_sb[0:1, 4:5], 1.4, 0.1, ALU.mult, ALU.add),
        inc=("vd", 1))
    P.op("vector", lambda e: e.scalar_tensor_tensor(
        scal_sb[0:1, 3:4], scal_sb[0:1, 1:2], -1.0, scal_sb[0:1, 4:5],
        ALU.mult, ALU.add), inc=("vd", 1))
    vd_scal = P.ctr["vd"]

    tox = []
    for ch in range(2):
        tok0 = ch * PC
        if ch > 0:
            P.wait("sync", "vd", P.ctr["vd"])
            S_ox.wait("sync", tox[-1])
        tx = [S_x.issue(lambda e, j=j, tok0=tok0: e.dma_start(
            out=xt[:, j * PC:(j + 1) * PC],
            in_=xT_d[j * DT:(j + 1) * DT, tok0:tok0 + PC])) for j in range(NDT)]
        th = [S_x.issue(lambda e, j=j, tok0=tok0: e.dma_start(
            out=outA[:, j * PC:(j + 1) * PC],
            in_=xhfT_d[j * DT:(j + 1) * DT, tok0:tok0 + PC])) for j in range(NDT)]
        for eng in ("vector", "scalar", "tensor"):
            S_x.wait(eng, th[-1])
        P.wait("vector", "vd", vd_scal)

        sl = lambda buf, j: buf[:, j * PC:(j + 1) * PC]
        cur, nxt = outA, outB
        for cyc in range(3):
            for j in range(NDT):
                P.op("vector", lambda e, j=j, cur=cur: e.tensor_copy(
                    sl(outb, j), sl(cur, j)), inc=("vd", 1))
            P.wait("tensor", "vd", P.ctr["vd"])
            P.wait("sync", "td", P.ctr["td"])
            td_w = []
            for i in range(NDT):
                buf = i % 2
                if i >= 2:
                    P.wait("sync", "td", td_w[i - 2])
                tw = S_w.issue(lambda e, i=i, buf=buf: e.dma_start(
                    out=wmm[:, buf * D:(buf + 1) * D].rearrange(
                        "p (j o) -> p j o", j=NDT),
                    in_=cw1_d[:, i * DT:(i + 1) * DT].rearrange(
                        "(j p) o -> p j o", p=DT)), parity=buf)
                S_w.wait("tensor", tw)
                for c in range(NSC):
                    for j in range(NDT):
                        P.op("tensor", lambda e, i=i, c=c, j=j, buf=buf: e.matmul(
                            ps_y[:, c * 512:(c + 1) * 512],
                            wmm[:, buf * D + j * DT: buf * D + (j + 1) * DT],
                            outb[:, j * PC + c * 512: j * PC + (c + 1) * 512],
                            start=(j == 0), stop=(j == NDT - 1)), inc=("td", 1))
                td_w.append(P.ctr["td"])
                P.wait("scalar", "td", P.ctr["td"])
                P.op("scalar", lambda e, i=i: e.activation(
                    c1[:, i * PC:(i + 1) * PC], ps_y[:, 0:PC], AF.Gelu,
                    bias=bias_cols[:, i:i + 1]), inc=("sd", 1))
                P.wait("tensor", "sd", P.ctr["sd"])
            P.wait("sync", "td", P.ctr["td"])
            for i in range(4):
                buf = i % 2
                if i >= 2:
                    P.wait("sync", "td", td_w[NDT - 2 + i - 1] if False else P.ctr["td"] * 0 + td_w2[i - 2])
                tw = S_w.issue(lambda e, i=i, buf=buf: e.dma_start(
                    out=wmm[:, buf * D:(buf + 1) * D].rearrange(
                        "p (j o) -> p j o", j=NDT),
                    in_=cw2_d[:, i * DT:(i + 1) * DT].rearrange(
                        "(j p) o -> p j o", p=DT)), parity=buf)
                S_w.wait("tensor", tw)
                for c in range(NSC):
                    for j in range(NDT):
                        P.op("tensor", lambda e, i=i, c=c, j=j, buf=buf: e.matmul(
                            ps_y[:, c * 512:(c + 1) * 512],
                            wmm[:, buf * D + j * DT: buf * D + (j + 1) * DT],
                            c1[:, j * PC + c * 512: j * PC + (c + 1) * 512],
                            start=(j == 0), stop=(j == NDT - 1)), inc=("td", 1))
                if i == 0:
                    td_w2 = []
                td_w2.append(P.ctr["td"])
                P.wait("scalar", "td", P.ctr["td"])
                P.op("scalar", lambda e, i=i: e.activation(
                    c2[:, i * PC:(i + 1) * PC], ps_y[:, 0:PC], AF.Gelu,
                    bias=bias_cols[:, NDT + i:NDT + i + 1]), inc=("sd", 1))
                P.wait("tensor", "sd", P.ctr["sd"])
            P.wait("sync", "td", P.ctr["td"])
            for i in range(2):
                buf = i % 2
                tw = S_w.issue(lambda e, i=i, buf=buf: e.dma_start(
                    out=wmm[:, buf * D: buf * D + 4 * DT].rearrange(
                        "p (j o) -> p j o", j=4),
                    in_=cw3_d[:, i * DT:(i + 1) * DT].rearrange(
                        "(j p) o -> p j o", p=DT)), parity=buf)
                S_w.wait("tensor", tw)
                for c in range(NSC):
                    for j in range(4):
                        P.op("tensor", lambda e, i=i, c=c, j=j, buf=buf: e.matmul(
                            ps_y[:, c * 512:(c + 1) * 512],
                            wmm[:, buf * D + j * DT: buf * D + (j + 1) * DT],
                            c2[:, j * PC + c * 512: j * PC + (c + 1) * 512],
                            start=(j == 0), stop=(j == 3)), inc=("td", 1))
                P.wait("scalar", "td", P.ctr["td"])
                P.op("scalar", lambda e, i=i: e.activation(
                    c3[:, i * PC:(i + 1) * PC], ps_y[:, 0:PC], AF.Gelu,
                    bias=bias_cols[:, NDT + 4 + i:NDT + 5 + i]), inc=("sd", 1))
                P.wait("tensor", "sd", P.ctr["sd"])
            for c in range(NSC):
                for j in range(2):
                    P.op("tensor", lambda e, c=c, j=j: e.matmul(
                        ps_y[0:1, c * 512:(c + 1) * 512], w4_sb[:, j:j + 1],
                        c3[:, j * PC + c * 512: j * PC + (c + 1) * 512],
                        start=(j == 0), stop=(j == 1)), inc=("td", 1))
            P.wait("scalar", "td", P.ctr["td"])
            P.op("scalar", lambda e: e.activation(
                tf_sb[0:1, :], ps_y[0:1, 0:PC], AF.Sigmoid,
                bias=b4g3_sb[0:1, 0:1]), inc=("sd", 1))
            sd_tf = P.ctr["sd"]
            P.wait("vector", "sd", sd_tf)
            P.op("vector", lambda e: e.tensor_scalar(
                tf_sb[0:1, :], tf_sb[0:1, :], scal_sb[0:1, 2:3],
                scal_sb[0:1, 3:4], ALU.mult, ALU.add), inc=("vd", 1))
            P.op("vector", lambda e: e.tensor_copy(cpb_sb[0:1, :], tf_sb[0:1, :]),
                 inc=("vd", 1))
            vd_cp = P.ctr["vd"]
            P.wait("tensor", "vd", vd_cp)
            for c in range(NSC):
                P.op("tensor", lambda e, c=c: e.matmul(
                    ps_bc[:, :], ones_row[0:1, 0:DT],
                    cpb_sb[0:1, c * 512:(c + 1) * 512], start=True, stop=True),
                    inc=("td", 1))
                P.wait("vector", "td", P.ctr["td"])
                for j in range(NDT):
                    sj = TWIST_SRC[j]
                    sg = TWIST_SGN[j]
                    P.op("vector", lambda e, j=j, sj=sj, c=c, cur=cur: e.tensor_tensor(
                        tmpc_sb[:, :], ps_bc[:, :],
                        cur[:, sj * PC + c * 512: sj * PC + (c + 1) * 512]
                        if False else cur[:, sj * PC + c * 512: sj * PC + (c + 1) * 512],
                        ALU.mult), inc=("vd", 1))
                    P.op("vector", lambda e, j=j, sg=sg, c=c, cur=cur, nxt=nxt:
                         e.scalar_tensor_tensor(
                             nxt[:, j * PC + c * 512: j * PC + (c + 1) * 512],
                             tmpc_sb[:, :], sg,
                             cur[:, j * PC + c * 512: j * PC + (c + 1) * 512],
                             ALU.mult, ALU.add), inc=("vd", 1))
                P.wait("tensor", "vd", P.ctr["vd"])
            cur, nxt = nxt, cur
        P.wait("vector", "vd", P.ctr["vd"])
        for j in range(NDT):
            P.op("vector", lambda e, j=j, cur=cur: e.scalar_tensor_tensor(
                xt[:, j * PC:(j + 1) * PC], sl(cur, j), 0.5,
                xt[:, j * PC:(j + 1) * PC], ALU.mult, ALU.add), inc=("vd", 1))
        P.wait("gpsimd", "vd", P.ctr["vd"])
        tox = [S_ox.issue(lambda e, j=j, tok0=tok0: e.dma_start(
            out=xoutT_o[j * DT:(j + 1) * DT, tok0:tok0 + PC],
            in_=xt[:, j * PC:(j + 1) * PC])) for j in range(NDT)]

    S_ox.wait("sync", tox[-1])
    return P





def build_launch_F(TF=T):
    SEG = 2048
    NSEG = TF // SEG
    NCH = TF // 128
    NW = NCH * 8
    CPS = 16
    NH = 8
    P = Prog("F")
    nc = P.nc

    zg_d = nc.dram_tensor("zg", [512, TF], BF16, kind="ExternalInput").ap()
    xbcg_d = nc.dram_tensor("xbcg", [640, TF + 3], BF16, kind="ExternalInput").ap()
    dtg_d = nc.dram_tensor("dtg", [128, NW], F32, kind="ExternalInput").ap()
    dtb_d = nc.dram_tensor("dtbias_t", [128, NW], F32, kind="ExternalInput").ap()
    alr_d = nc.dram_tensor("alog_rep", [128, 8], F32, kind="ExternalInput").ap()
    al8_d = nc.dram_tensor("alog8r", [8, 128], F32, kind="ExternalInput").ap()
    dc8_d = nc.dram_tensor("dcol8", [128, 8], F32, kind="ExternalInput").ap()
    cdx_d = nc.dram_tensor("convdx", [16, 128, 128], BF16, kind="ExternalInput").ap()
    cdb_d = nc.dram_tensor("convdb", [4, 64, 64], BF16, kind="ExternalInput").ap()
    cdc_d = nc.dram_tensor("convdc", [4, 64, 64], BF16, kind="ExternalInput").ap()
    cbx_d = nc.dram_tensor("convbx", [128, 4], F32, kind="ExternalInput").ap()
    cbb_d = nc.dram_tensor("convbb", [64, 1], F32, kind="ExternalInput").ap()
    cbc_d = nc.dram_tensor("convbc", [64, 1], F32, kind="ExternalInput").ap()
    tri_d = nc.dram_tensor("triu128", [128, 128], BF16, kind="ExternalInput").ap()
    idn_d = nc.dram_tensor("iden128", [128, 128], BF16, kind="ExternalInput").ap()
    i8t_d = nc.dram_tensor("i8t", [8, 512], BF16, kind="ExternalInput").ap()
    scr_d = nc.dram_tensor("scrF", [3, 128, 128], BF16, kind="Internal").ap()
    ygT_o = nc.dram_tensor("ygT", [512, TF], BF16, kind="ExternalOutput").ap()
    ssq_o = nc.dram_tensor("ssq", [128, NCH], F32, kind="ExternalOutput").ap()

    P.register_consts([0.0, 30.0, -30.0, -1.0, 1.0])

    SG3 = SEG + 3
    zraw = P.sb("zraw", [128, 4 * SEG], BF16)
    xbcs = P.sb("xbcs", [128, 4 * SG3], BF16)
    bslab = P.sb("bslab", [64, SG3], BF16)
    cslab = P.sb("cslab", [64, SG3], BF16)
    xsil = P.sb("xsil", [128, 4 * SEG], BF16)
    bsil = P.sb("bsil", [64, SEG], BF16)
    csil = P.sb("csil", [64, SEG], BF16)
    ygst = P.sb("ygst", [128, 4 * SEG], BF16)
    stackL = P.sb("stackL", [6, CPS * 8 * 128], BF16)
    stackR = P.sb("stackR", [6, CPS * 8 * 128], BF16)
    dtbuf = P.sb("dtbuf", [128, NW])
    dtbias = P.sb("dtbias", [128, NW])
    tmpa = P.sb("tmpa", [128, NW])
    tmpb = P.sb("tmpb", [128, NW])
    dtf = P.sb("dtf", [128, NW])
    dthi = P.sb("dthi", [128, NW], BF16)
    dtlo = P.sb("dtlo", [128, NW], BF16)
    csPf = P.sb("csPf", [128, NW])
    cshi = P.sb("cshi", [128, NW], BF16)
    csmid = P.sb("csmid", [128, NW], BF16)
    cslo = P.sb("cslo", [128, NW], BF16)
    esP = P.sb("esP", [128, NW])
    ascL = P.sb("ascL", [1, NW])
    ascLhi = P.sb("ascLhi", [1, NW], BF16)
    ascLlo = P.sb("ascLlo", [1, NW], BF16)
    rowt = P.sb("rowt", [1, NW])
    dAall = P.sb("dAall", [64, NW])
    alog = P.sb("alogb", [128, 8])
    Acol8 = P.sb("Acol8", [128, 8])
    al8f = P.sb("al8f", [8, 128])
    a8rb = P.sb("a8rb", [8, 128], BF16)
    a8tmp = P.sb("a8tmp", [8, 128])
    dcol = P.sb("dcol", [128, 8])
    cdx_sb = P.sb("cdx_sb", [128, 16 * 128], BF16)
    cdb_sb = P.sb("cdb_sb", [64, 4 * 64], BF16)
    cdc_sb = P.sb("cdc_sb", [64, 4 * 64], BF16)
    cbx_sb = P.sb("cbx_sb", [128, 4])
    cbb_sb = P.sb("cbb_sb", [64, 1])
    cbc_sb = P.sb("cbc_sb", [64, 1])
    tri_sb = P.sb("tri_sb", [128, 128], BF16)
    idn_sb = P.sb("idn_sb", [128, 128], BF16)
    i8t_sb = P.sb("i8t_sb", [8, 512], BF16)
    trsp = P.sb("trsp", [128, 3 * 128], BF16)
    h_f = P.sb("h_f", [64, 512])
    h_b = P.sb("h_b", [64, 512], BF16)
    cbsb = P.sb("cbsb", [128, 128], BF16)
    btb = P.sb("btb", [128, 64], BF16)
    xtb = P.sb("xtb", [128, 512], BF16)
    ztb = P.sb("ztb", [128, 512], BF16)
    dtxb = P.sb("dtxb", [128, 512], BF16)
    eBb = P.sb("eBb", [128, 512], BF16)
    adf = P.sb("adf", [128, NH * 128])
    ad2 = P.sb("ad2", [128, NH * 128])
    eLb = P.sb("eLb", [128, NH * 128], BF16)
    e2c = P.sb("e2c", [128, 8])
    Gb = P.sb("Gb", [128, NH * 128], BF16)
    ysc = P.sb("ysc", [128, 64])
    y2 = P.sb("y2", [128, 64])
    yTc = P.sb("yTc", [128, 512])
    ygf = P.sb("ygf", [128, 512])
    sscol = P.sb("sscol", [128, NCH])
    ones1 = P.sb("ones1", [1, 64], BF16)

    ps_a = P.psum("ps_a", [128, 512])
    ps_bc2 = P.psum("ps_bc2", [128, 1024])
    ps_y = P.psum("ps_y", [128, 512])
    ps_yin = P.psum("ps_yin", [128, 512])
    ps_h = P.psum("ps_h", [64, 512])
    ps_tr = P.psum("ps_tr", [128, 512])
    ps_cb = P.psum("ps_cb", [128, 256])

    for s in ["vd", "sd", "td", "cd"]:
        P.sem(s)
    S_p = DmaStream(P, "lp", "sync", 1)
    S_x = DmaStream(P, "lx", "sync", 2)
    S_st = DmaStream(P, "st", "gpsimd", 1)
    S_o = DmaStream(P, "go", "gpsimd", 1)

    P.emit_const_memsets("gpsimd", ("cd", 1))
    cd_init = P.ctr["cd"]

    for fn in [
        lambda e: e.dma_start(out=dtbuf[:, :], in_=dtg_d),
        lambda e: e.dma_start(out=dtbias[:, :], in_=dtb_d),
        lambda e: e.dma_start(out=alog[:, :], in_=alr_d),
        lambda e: e.dma_start(out=al8f[:, :], in_=al8_d),
        lambda e: e.dma_start(out=dcol[:, :], in_=dc8_d),
        lambda e: e.dma_start(out=cdx_sb[:, :].rearrange("p (g f) -> p g f", g=16),
                              in_=cdx_d.rearrange("g p f -> p g f")),
        lambda e: e.dma_start(out=cdb_sb[:, :].rearrange("p (g f) -> p g f", g=4),
                              in_=cdb_d.rearrange("g p f -> p g f")),
        lambda e: e.dma_start(out=cdc_sb[:, :].rearrange("p (g f) -> p g f", g=4),
                              in_=cdc_d.rearrange("g p f -> p g f")),
        lambda e: e.dma_start(out=cbx_sb[:, :], in_=cbx_d),
        lambda e: e.dma_start(out=cbb_sb[:, :], in_=cbb_d),
        lambda e: e.dma_start(out=cbc_sb[:, :], in_=cbc_d),
        lambda e: e.dma_start(out=tri_sb[:, :], in_=tri_d),
        lambda e: e.dma_start(out=idn_sb[:, :], in_=idn_d),
        lambda e: e.dma_start(out=i8t_sb[:, :], in_=i8t_d),
    ]:
        tok_p = S_p.issue(fn)

    P.wait("vector", "cd", cd_init)
    P.wait("scalar", "cd", cd_init)
    P.wait("tensor", "cd", cd_init)
    P.op("vector", lambda e: e.memset(stackL[:, :], 1.0), inc=("vd", 1))
    P.op("vector", lambda e: e.memset(stackR[:, :], -1.0), inc=("vd", 1))
    P.op("vector", lambda e: e.memset(h_f[:, :], 0.0), inc=("vd", 1))
    P.op("vector", lambda e: e.memset(h_b[:, :], 0.0), inc=("vd", 1))
    P.op("vector", lambda e: e.memset(ones1[:, :], 1.0), inc=("vd", 1))

    for eng in ("vector", "scalar", "tensor"):
        S_p.wait(eng, tok_p)

    P.op("scalar", lambda e: e.activation(Acol8[:, :], alog[:, :], AF.Exp),
         inc=("sd", 1))
    P.op("scalar", lambda e: e.activation(a8tmp[:, :], al8f[:, :], AF.Exp),
         inc=("sd", 1))
    P.wait("vector", "sd", P.ctr["sd"])
    P.op("vector", lambda e: e.tensor_scalar(
        Acol8[:, :], Acol8[:, :], -1.0, None, ALU.mult), inc=("vd", 1))
    P.op("vector", lambda e: e.tensor_scalar(
        a8tmp[:, :], a8tmp[:, :], -1.0, None, ALU.mult), inc=("vd", 1))
    P.op("vector", lambda e: e.tensor_copy(a8rb[:, :], a8tmp[:, :]), inc=("vd", 1))

    P.op("vector", lambda e: e.tensor_tensor(
        dtbuf[:, :], dtbuf[:, :], dtbias[:, :], ALU.add), inc=("vd", 1))
    P.op("vector", lambda e: e.tensor_scalar(
        tmpa[:, :], dtbuf[:, :], 0.0, None, ALU.max), inc=("vd", 1))
    P.wait("scalar", "vd", P.ctr["vd"])
    P.op("scalar", lambda e: e.activation(tmpb[:, :], dtbuf[:, :], AF.Exp),
         inc=("sd", 1))
    P.op("scalar", lambda e: e.activation(dtf[:, :], dtbuf[:, :], AF.Exp,
                                          scale=-1.0), inc=("sd", 1))
    P.wait("vector", "sd", P.ctr["sd"])
    P.op("vector", lambda e: e.tensor_tensor(
        tmpb[:, :], tmpb[:, :], dtf[:, :], ALU.min), inc=("vd", 1))
    P.wait("scalar", "vd", P.ctr["vd"])
    P.op("scalar", lambda e: e.activation(tmpb[:, :], tmpb[:, :], AF.Ln,
                                          bias=1.0), inc=("sd", 1))
    P.wait("vector", "sd", P.ctr["sd"])
    P.op("vector", lambda e: e.tensor_tensor(
        dtf[:, :], tmpa[:, :], tmpb[:, :], ALU.add), inc=("vd", 1))
    P.op("vector", lambda e: e.tensor_copy(dthi[:, :], dtf[:, :]), inc=("vd", 1))
    P.op("vector", lambda e: e.tensor_tensor(
        tmpa[:, :], dtf[:, :], dthi[:, :], ALU.subtract), inc=("vd", 1))
    P.op("vector", lambda e: e.tensor_copy(dtlo[:, :], tmpa[:, :]), inc=("vd", 1))
    vd_dt = P.ctr["vd"]

    P.wait("tensor", "vd", vd_dt)
    P.op("tensor", lambda e: e.matmul(
        ps_a[:, 0:NW], tri_sb[:, :], dthi[:, :], start=True, stop=False),
        inc=("td", 1))
    P.op("tensor", lambda e: e.matmul(
        ps_a[:, 0:NW], tri_sb[:, :], dtlo[:, :], start=False, stop=True),
        inc=("td", 1))
    P.wait("vector", "td", P.ctr["td"])
    P.op("vector", lambda e: e.tensor_copy(csPf[:, :], ps_a[:, 0:NW]),
         inc=("vd", 1))
    P.op("vector", lambda e: e.tensor_copy(cshi[:, :], csPf[:, :]), inc=("vd", 1))
    P.op("vector", lambda e: e.tensor_tensor(
        tmpa[:, :], csPf[:, :], cshi[:, :], ALU.subtract), inc=("vd", 1))
    P.op("vector", lambda e: e.tensor_copy(csmid[:, :], tmpa[:, :]), inc=("vd", 1))
    P.op("vector", lambda e: e.tensor_tensor(
        tmpb[:, :], tmpa[:, :], csmid[:, :], ALU.subtract), inc=("vd", 1))
    P.op("vector", lambda e: e.tensor_copy(cslo[:, :], tmpb[:, :]), inc=("vd", 1))
    vd_cs = P.ctr["vd"]

    P.wait("tensor", "vd", vd_cs)
    P.op("tensor", lambda e: e.matmul(
        ps_a[:, 0:NW], a8rb[:, :], i8t_sb[:, 0:NW], start=True, stop=True),
        inc=("td", 1))
    P.wait("vector", "td", P.ctr["td"])
    P.op("vector", lambda e: e.tensor_tensor(
        tmpa[:, :], csPf[:, :], ps_a[:, 0:NW], ALU.mult), inc=("vd", 1))
    P.wait("scalar", "vd", P.ctr["vd"])
    P.op("scalar", lambda e: e.activation(esP[:, :], tmpa[:, :], AF.Exp),
         inc=("sd", 1))
    P.wait("gpsimd", "vd", P.ctr["vd"])
    tok_al = S_st.issue(lambda e: e.dma_start(
        out=ascL[:, :], in_=tmpa[127:128, :]))
    S_st.wait("vector", tok_al)
    P.op("vector", lambda e: e.tensor_copy(ascLhi[:, :], ascL[:, :]), inc=("vd", 1))
    P.op("vector", lambda e: e.tensor_tensor(
        rowt[:, :], ascL[:, :], ascLhi[:, :], ALU.subtract), inc=("vd", 1))
    P.op("vector", lambda e: e.tensor_copy(ascLlo[:, :], rowt[:, :]), inc=("vd", 1))
    P.wait("tensor", "vd", P.ctr["vd"])
    P.op("tensor", lambda e: e.matmul(
        ps_h[0:64, 0:NW], ones1[:, :], ascLhi[:, :], start=True, stop=False),
        inc=("td", 1))
    P.op("tensor", lambda e: e.matmul(
        ps_h[0:64, 0:NW], ones1[:, :], ascLlo[:, :], start=False, stop=True),
        inc=("td", 1))
    P.wait("scalar", "td", P.ctr["td"])
    P.op("scalar", lambda e: e.activation(dAall[:, :], ps_h[0:64, 0:NW], AF.Exp),
         inc=("sd", 1))
    P.wait("vector", "sd", P.ctr["sd"])
    P.wait("tensor", "sd", P.ctr["sd"])

    tok_yg = None
    tok_st = None
    vd_p3_prev = P.ctr["vd"]
    sd_z_prev = P.ctr["sd"]
    td_yg_prev = P.ctr["td"]
    sd_eL_prev = P.ctr["sd"]
    for s in range(NSEG):
        P.wait("sync", "td", P.ctr["td"])
        P.wait("sync", "sd", P.ctr["sd"])
        tox = []
        for g in range(4):
            tox.append(S_x.issue(lambda e, g=g, s=s: e.dma_start(
                out=xbcs[:, g * SG3:(g + 1) * SG3],
                in_=xbcg_d[g * 128:(g + 1) * 128, s * SEG: s * SEG + SG3]),
                parity=0))
        tox.append(S_x.issue(lambda e, s=s: e.dma_start(
            out=bslab[:, :], in_=xbcg_d[512:576, s * SEG: s * SEG + SG3]),
            parity=0))
        tox.append(S_x.issue(lambda e, s=s: e.dma_start(
            out=cslab[:, :], in_=xbcg_d[576:640, s * SEG: s * SEG + SG3]),
            parity=0))
        toz = []
        for g in range(4):
            toz.append(S_x.issue(lambda e, g=g, s=s: e.dma_start(
                out=zraw[:, g * SEG:(g + 1) * SEG],
                in_=zg_d[g * 128:(g + 1) * 128, s * SEG:(s + 1) * SEG]),
                parity=1))
        S_x.wait("tensor", tox[-1])
        S_x.wait("scalar", toz[-1])
        P.wait("tensor", "vd", P.ctr["vd"])
        P.wait("tensor", "sd", P.ctr["sd"])
        sd_cw = {}
        cwi = 0
        for g in range(4):
            for w in range(4):
                pcv = ps_a if cwi % 2 == 0 else ps_tr
                if cwi >= 2:
                    P.wait("tensor", "sd", sd_cw[cwi - 2])
                for k in range(4):
                    P.op("tensor", lambda e, g=g, w=w, k=k, pcv=pcv: e.matmul(
                        pcv[:, 0:512],
                        cdx_sb[:, (g * 4 + k) * 128:(g * 4 + k + 1) * 128],
                        xbcs[:, g * SG3 + w * 512 + k: g * SG3 + w * 512 + k + 512],
                        start=(k == 0), stop=(k == 3)), inc=("td", 1))
                P.wait("scalar", "td", P.ctr["td"])
                P.op("scalar", lambda e, g=g, w=w, pcv=pcv: e.activation(
                    xsil[:, g * SEG + w * 512: g * SEG + w * 512 + 512],
                    pcv[:, 0:512], AF.Silu, bias=cbx_sb[:, g:g + 1]),
                    inc=("sd", 1))
                sd_cw[cwi] = P.ctr["sd"]
                cwi += 1
        for bc in range(2):
            wsb = cdb_sb if bc == 0 else cdc_sb
            slab = bslab if bc == 0 else cslab
            bias = cbb_sb if bc == 0 else cbc_sb
            dst = bsil if bc == 0 else csil
            for w in range(4):
                pcv = ps_a if cwi % 2 == 0 else ps_tr
                P.wait("tensor", "sd", sd_cw[cwi - 2])
                for k in range(4):
                    P.op("tensor", lambda e, w=w, k=k, wsb=wsb, slab=slab,
                         pcv=pcv: e.matmul(
                        pcv[0:64, 0:512], wsb[:, k * 64:(k + 1) * 64],
                        slab[:, w * 512 + k: w * 512 + k + 512],
                        start=(k == 0), stop=(k == 3)), inc=("td", 1))
                P.wait("scalar", "td", P.ctr["td"])
                P.op("scalar", lambda e, w=w, bias=bias, dst=dst, pcv=pcv:
                     e.activation(
                        dst[:, w * 512:(w + 1) * 512], pcv[0:64, 0:512],
                        AF.Silu, bias=bias[:, 0:1]), inc=("sd", 1))
                sd_cw[cwi] = P.ctr["sd"]
                cwi += 1
        P.wait("tensor", "sd", P.ctr["sd"])
        if tok_st is not None:
            S_st.wait("gpsimd", tok_st)
            S_st.wait("vector", tok_st)
        for i, buf in enumerate((cshi, csmid, cslo)):
            P.op("tensor", lambda e, i=i, buf=buf, s=s: e.matmul(
                ps_tr[:, i * 128:(i + 1) * 128], buf[:, s * 128:(s + 1) * 128],
                idn_sb[:, :], start=True, stop=True), inc=("td", 1))
        P.wait("vector", "td", P.ctr["td"])
        P.op("vector", lambda e: e.tensor_copy(trsp[:, :], ps_tr[:, 0:384]),
             inc=("vd", 1))
        P.wait("gpsimd", "vd", P.ctr["vd"])
        P.wait("gpsimd", "td", P.ctr["td"])
        tok_w = None
        for i in range(3):
            tok_w = S_st.issue(lambda e, i=i: e.dma_start(
                out=scr_d[i], in_=trsp[:, i * 128:(i + 1) * 128]))
        S_st.wait("gpsimd", tok_w)
        for i in range(3):
            S_st.issue(lambda e, i=i: e.dma_start(
                out=stackL[3 + i:4 + i, :],
                in_=scr_d[i].rearrange("p f -> () (p f)")))
        for i in range(3):
            tok_st = S_st.issue(lambda e, i=i: e.dma_start(
                out=stackR[i:i + 1, :],
                in_=scr_d[i].rearrange("p f -> () (p f)")))
        S_st.wait("tensor", tok_st)
        if tok_yg is not None:
            S_o.wait("vector", tok_yg)

        for n in range(CPS):
            N = s * CPS + n
            P.wait("tensor", "vd", vd_p3_prev)
            P.wait("tensor", "sd", sd_z_prev)
            P.op("tensor", lambda e, n=n: e.matmul(
                ps_cb[:, 0:128], bsil[:, n * 128:(n + 1) * 128],
                csil[:, n * 128:(n + 1) * 128], start=True, stop=True),
                inc=("td", 1))
            P.op("tensor", lambda e, n=n: e.matmul(
                ps_cb[:, 128:192], bsil[:, n * 128:(n + 1) * 128],
                idn_sb[0:64, 0:64], start=True, stop=True), inc=("td", 1))
            for g in range(4):
                P.op("tensor", lambda e, g=g, n=n: e.matmul(
                    ps_tr[:, g * 128:(g + 1) * 128],
                    xsil[:, g * SEG + n * 128: g * SEG + (n + 1) * 128],
                    idn_sb[:, :], start=True, stop=True), inc=("td", 1))
            for h in range(NH):
                r = n * 8 + h
                P.op("tensor", lambda e, r=r, h=h: e.matmul(
                    ps_bc2[:, h * 128:(h + 1) * 128],
                    stackL[:, r * 128:(r + 1) * 128],
                    stackR[:, r * 128:(r + 1) * 128], start=True, stop=True),
                    inc=("td", 1))
            td_tr = P.ctr["td"]
            P.wait("vector", "td", td_tr)
            P.op("vector", lambda e: e.tensor_copy(cbsb[:, :], ps_cb[:, 0:128]),
                 inc=("vd", 1))
            P.op("vector", lambda e: e.tensor_copy(btb[:, :], ps_cb[:, 128:192]),
                 inc=("vd", 1))
            P.op("vector", lambda e: e.tensor_copy(xtb[:, :], ps_tr[:, 0:512]),
                 inc=("vd", 1))
            vd_ev1 = P.ctr["vd"]
            P.wait("tensor", "vd", vd_ev1)
            for g in range(4):
                P.op("tensor", lambda e, g=g, n=n: e.matmul(
                    ps_tr[:, g * 128:(g + 1) * 128],
                    zraw[:, g * SEG + n * 128: g * SEG + (n + 1) * 128],
                    idn_sb[:, :], start=True, stop=True), inc=("td", 1))
            P.wait("scalar", "td", P.ctr["td"])
            P.op("scalar", lambda e: e.activation(
                ztb[:, :], ps_tr[:, 0:512], AF.Silu), inc=("sd", 1))
            sd_z_cur = P.ctr["sd"]
            P.wait("vector", "sd", sd_eL_prev)
            vd_ad = {}
            for h in range(NH):
                P.op("vector", lambda e, h=h: e.tensor_scalar(
                    adf[:, h * 128:(h + 1) * 128],
                    ps_bc2[:, h * 128:(h + 1) * 128],
                    Acol8[:, h:h + 1], None, ALU.mult), inc=("vd", 1))
                P.op("vector", lambda e, h=h: e.scalar_tensor_tensor(
                    ad2[:, h * 128:(h + 1) * 128],
                    adf[:, h * 128:(h + 1) * 128], 30.0, tri_sb[:, :],
                    ALU.add, ALU.mult), inc=("vd", 1))
                vd_ad[h] = P.ctr["vd"]
            sd_eL = {}
            for h in range(NH):
                P.wait("scalar", "vd", vd_ad[h])
                P.op("scalar", lambda e, h=h: e.activation(
                    eLb[:, h * 128:(h + 1) * 128],
                    ad2[:, h * 128:(h + 1) * 128], AF.Exp, bias=-30.0),
                    inc=("sd", 1))
                P.op("scalar", lambda e, h=h: e.activation(
                    e2c[:, h:h + 1], ad2[:, h * 128 + 127: h * 128 + 128],
                    AF.Exp, bias=-30.0), inc=("sd", 1))
                sd_eL[h] = P.ctr["sd"]
            sd_eL_prev = sd_eL[NH - 1]
            P.wait("vector", "td", td_yg_prev)
            for h in range(NH):
                P.wait("vector", "sd", sd_eL[h])
                P.op("vector", lambda e, h=h: e.tensor_tensor(
                    Gb[:, h * 128:(h + 1) * 128], eLb[:, h * 128:(h + 1) * 128],
                    cbsb[:, :], ALU.mult), inc=("vd", 1))
                P.op("vector", lambda e, h=h, N=N: e.tensor_scalar(
                    dtxb[:, h * 64:(h + 1) * 64], xtb[:, h * 64:(h + 1) * 64],
                    dtf[:, N * 8 + h: N * 8 + h + 1], None, ALU.mult),
                    inc=("vd", 1))
                P.op("vector", lambda e, h=h: e.tensor_scalar(
                    eBb[:, h * 64:(h + 1) * 64], btb[:, :],
                    e2c[:, h:h + 1], None, ALU.mult), inc=("vd", 1))
            vd_p2 = P.ctr["vd"]
            P.wait("tensor", "vd", vd_p2)
            for h in range(NH):
                P.op("tensor", lambda e, h=h, n=n: e.matmul(
                    ps_yin[:, h * 64:(h + 1) * 64],
                    csil[:, n * 128:(n + 1) * 128], h_b[:, h * 64:(h + 1) * 64],
                    start=True, stop=True), inc=("td", 1))
                P.op("tensor", lambda e, h=h: e.matmul(
                    ps_y[:, h * 64:(h + 1) * 64], Gb[:, h * 128:(h + 1) * 128],
                    dtxb[:, h * 64:(h + 1) * 64], start=True, stop=True),
                    inc=("td", 1))
                P.op("tensor", lambda e, h=h: e.matmul(
                    ps_h[0:64, h * 64:(h + 1) * 64], eBb[:, h * 64:(h + 1) * 64],
                    dtxb[:, h * 64:(h + 1) * 64], start=True, stop=True),
                    inc=("td", 1))
            td_y = P.ctr["td"]
            P.wait("vector", "td", td_y)
            for h in range(NH):
                P.op("vector", lambda e, h=h, N=N: e.tensor_scalar(
                    ysc[:, :], ps_yin[:, h * 64:(h + 1) * 64],
                    esP[:, N * 8 + h: N * 8 + h + 1], None, ALU.mult),
                    inc=("vd", 1))
                P.op("vector", lambda e, h=h: e.tensor_tensor(
                    y2[:, :], ysc[:, :], ps_y[:, h * 64:(h + 1) * 64], ALU.add),
                    inc=("vd", 1))
                P.op("vector", lambda e, h=h: e.scalar_tensor_tensor(
                    yTc[:, h * 64:(h + 1) * 64], xtb[:, h * 64:(h + 1) * 64],
                    dcol[:, h:h + 1], y2[:, :], ALU.mult, ALU.add),
                    inc=("vd", 1))
                P.op("vector", lambda e, h=h, N=N: e.scalar_tensor_tensor(
                    h_f[:, h * 64:(h + 1) * 64], h_f[:, h * 64:(h + 1) * 64],
                    dAall[:, N * 8 + h: N * 8 + h + 1],
                    ps_h[0:64, h * 64:(h + 1) * 64], ALU.mult, ALU.add),
                    inc=("vd", 1))
                P.op("vector", lambda e, h=h: e.tensor_copy(
                    h_b[:, h * 64:(h + 1) * 64], h_f[:, h * 64:(h + 1) * 64]),
                    inc=("vd", 1))
            vd_p3_prev = P.ctr["vd"]
            P.wait("vector", "sd", sd_z_cur)
            sd_z_prev = sd_z_cur
            P.op("vector", lambda e: e.tensor_tensor(
                ygf[:, :], yTc[:, :], ztb[:, :], ALU.mult), inc=("vd", 1))
            P.op("vector", lambda e: e.tensor_tensor(
                yTc[:, :], ygf[:, :], ygf[:, :], ALU.mult), inc=("vd", 1))
            P.op("vector", lambda e, N=N: e.reduce_sum(
                sscol[:, N:N + 1], yTc[:, :], axis=mybir.AxisListType.X),
                inc=("vd", 1))
            P.op("vector", lambda e: e.tensor_copy(dtxb[:, :], ygf[:, :]),
                 inc=("vd", 1))
            vd_g = P.ctr["vd"]
            P.wait("tensor", "vd", vd_g)
            for g in range(4):
                P.op("tensor", lambda e, g=g: e.matmul(
                    ps_a[:, g * 128:(g + 1) * 128], dtxb[:, g * 128:(g + 1) * 128],
                    idn_sb[:, :], start=True, stop=True), inc=("td", 1))
            td_yg_prev = P.ctr["td"]
            P.wait("vector", "td", td_yg_prev)
            for g in range(4):
                P.op("vector", lambda e, g=g, n=n: e.tensor_copy(
                    ygst[:, g * SEG + n * 128: g * SEG + (n + 1) * 128],
                    ps_a[:, g * 128:(g + 1) * 128]), inc=("vd", 1))
        P.wait("gpsimd", "vd", P.ctr["vd"])
        for g in range(4):
            tok_yg = S_o.issue(lambda e, g=g, s=s: e.dma_start(
                out=ygT_o[g * 128:(g + 1) * 128, s * SEG:(s + 1) * SEG],
                in_=ygst[:, g * SEG:(g + 1) * SEG]))

    P.wait("gpsimd", "vd", P.ctr["vd"])
    tok_ss = S_o.issue(lambda e: e.dma_start(out=ssq_o, in_=sscol[:, :]))
    S_o.wait("sync", tok_ss)
    return P



def build_launch_E():
    P = Prog("E")
    nc = P.nc

    xT_d = nc.dram_tensor("xT", [D, CT], F32, kind="ExternalInput").ap()
    WinT_d = nc.dram_tensor("WinT", [D, 4256], BF16, kind="ExternalInput").ap()
    lnw_d = nc.dram_tensor("m_ln_w_c", [DT, NDT], F32, kind="ExternalInput").ap()
    lnb_d = nc.dram_tensor("m_ln_b_c", [DT, NDT], F32, kind="ExternalInput").ap()

    xh4fT_o = nc.dram_tensor("xh4fT", [D, CT], F32, kind="ExternalOutput").ap()
    zT_o = nc.dram_tensor("zT", [2048, CT], BF16, kind="ExternalOutput").ap()
    xBCT_o = nc.dram_tensor("xBCT", [2176, CT], BF16, kind="ExternalOutput").ap()
    dtr_o = nc.dram_tensor("dtr", [32, CT], F32, kind="ExternalOutput").ap()

    P.register_consts([1.0 / 1024.0, EPS_LN, -0.5])

    xt = P.sb("xt", [DT, NDT * PC])
    xinf = P.sb("xinf", [DT, NDT * PC])
    castb = P.sb("castb", [DT, NDT * PC], BF16)
    ub = P.sb("ub", [DT, NDT * PC], BF16)
    sqs = P.sb("sqs", [DT, 2 * 512], BF16)
    wmm = P.sb("wmm", [DT, 2 * D], BF16)
    rows = P.sb("rows", [1, 2 * 512])
    rowb = P.sb("rowb", [1, 2 * PC], BF16)
    ones_col = P.sb("ones_col", [DT, 1], BF16)
    ones_row = P.sb("ones_row", [1, PC], BF16)
    lnw_sb = P.sb("lnw_sb", [DT, NDT])
    lnb_sb = P.sb("lnb_sb", [DT, NDT])
    ev = P.sb("ev", [DT, PC], BF16)
    evf = P.sb("evf", [DT, PC])

    ps_y = P.psum("ps_y", [DT, PC])
    ps_bc = P.psum("ps_bc", [DT, PC])
    ps_row = P.psum("ps_row", [33, 512])

    for s in ["vd", "sd", "td", "cd"]:
        P.sem(s)
    S_param = DmaStream(P, "lp", "sync", 1)
    S_w = DmaStream(P, "lw", "sync", 2)
    S_x = DmaStream(P, "lx", "sync", 1)
    S_oh = DmaStream(P, "gh", "gpsimd", 1)
    S_oz = DmaStream(P, "gz", "gpsimd", 1)

    P.emit_const_memsets("gpsimd", ("cd", 1))
    cd_init = P.ctr["cd"]

    S_param.issue(lambda e: e.dma_start(out=lnw_sb[:, :], in_=lnw_d))
    tok_params = S_param.issue(lambda e: e.dma_start(out=lnb_sb[:, :], in_=lnb_d))
    P.wait("vector", "cd", cd_init)
    P.op("vector", lambda e: e.memset(ones_col[:, :], 1.0), inc=("vd", 1))
    P.op("vector", lambda e: e.memset(ones_row[:, :], 1.0), inc=("vd", 1))
    vd_ones = P.ctr["vd"]

    ctx = dict(P=P, sqs=sqs, rows=rows, rowb=rowb, ones_col=ones_col,
               ones_row=ones_row, ps_row=ps_row, ps_bc=ps_bc)

    toh = []
    for ch in range(2):
        tok0 = ch * PC
        if ch > 0:
            P.wait("sync", "vd", P.ctr["vd"])
            P.wait("sync", "sd", P.ctr["sd"])
            S_oh.wait("sync", toh[-1])
        tox = [S_x.issue(lambda e, j=j, tok0=tok0: e.dma_start(
            out=xt[:, j * PC:(j + 1) * PC],
            in_=xT_d[j * DT:(j + 1) * DT, tok0:tok0 + PC])) for j in range(NDT)]
        for eng in ("vector", "scalar", "tensor"):
            S_x.wait(eng, tox[-1])
            S_param.wait(eng, tok_params)
        P.wait("vector", "vd", vd_ones)
        sl = lambda buf, j: buf[:, j * PC:(j + 1) * PC]
        emit_panel_ln(ctx, lambda j: sl(xt, j), lambda j: sl(castb, j),
                      lambda j: sl(xinf, j), lambda j: sl(castb, j), EPS_LN)
        P.wait("gpsimd", "vd", P.ctr["vd"])
        toh = [S_oh.issue(lambda e, j=j, tok0=tok0: e.dma_start(
            out=xh4fT_o[j * DT:(j + 1) * DT, tok0:tok0 + PC],
            in_=xinf[:, j * PC:(j + 1) * PC])) for j in range(NDT)]
        emit_panel_ln(ctx, lambda j: sl(xinf, j), lambda j: sl(castb, j),
                      lambda j: sl(xt, j), lambda j: sl(ub, j), EPS_LN,
                      gamma_ap=lambda j: lnw_sb[:, j:j + 1],
                      beta_ap=lambda j: lnb_sb[:, j:j + 1],
                      act_func=AF.Copy)
        P.wait("sync", "td", P.ctr["td"])
        td_w = []
        toz = None
        for i in range(34):
            buf = i % 2
            ow = 32 if i == 33 else DT
            if i >= 2:
                P.wait("sync", "td", td_w[i - 2])
            tw = S_w.issue(lambda e, i=i, buf=buf, ow=ow: e.dma_start(
                out=wmm[:, buf * D: buf * D + NDT * ow].rearrange(
                    "p (j o) -> p j o", j=NDT),
                in_=WinT_d[:, i * DT: i * DT + ow].rearrange(
                    "(j p) o -> p j o", p=DT)), parity=buf)
            S_w.wait("tensor", tw)
            for c in range(NSC):
                for j in range(NDT):
                    P.op("tensor", lambda e, i=i, c=c, j=j, buf=buf, ow=ow: e.matmul(
                        ps_y[0:ow, c * 512:(c + 1) * 512],
                        wmm[:, buf * D + j * ow: buf * D + (j + 1) * ow],
                        ub[:, j * PC + c * 512: j * PC + (c + 1) * 512],
                        start=(j == 0), stop=(j == NDT - 1)), inc=("td", 1))
            td_w.append(P.ctr["td"])
            if toz is not None:
                S_oz.wait("scalar", toz)
                S_oz.wait("vector", toz)
            P.wait("scalar", "td", P.ctr["td"])
            P.wait("vector", "td", P.ctr["td"])
            if i < 16:
                P.op("scalar", lambda e: e.activation(
                    ev[:, :], ps_y[:, 0:PC], AF.Copy), inc=("sd", 1))
                P.wait("gpsimd", "sd", P.ctr["sd"])
                toz = S_oz.issue(lambda e, i=i, tok0=tok0: e.dma_start(
                    out=zT_o[i * DT:(i + 1) * DT, tok0:tok0 + PC],
                    in_=ev[:, :]))
            elif i < 33:
                P.op("scalar", lambda e: e.activation(
                    ev[:, :], ps_y[:, 0:PC], AF.Copy), inc=("sd", 1))
                P.wait("gpsimd", "sd", P.ctr["sd"])
                toz = S_oz.issue(lambda e, i=i, tok0=tok0: e.dma_start(
                    out=xBCT_o[(i - 16) * DT:(i - 15) * DT, tok0:tok0 + PC],
                    in_=ev[:, :]))
            else:
                P.op("vector", lambda e: e.tensor_copy(
                    evf[0:32, :], ps_y[0:32, 0:PC]), inc=("vd", 1))
                P.wait("gpsimd", "vd", P.ctr["vd"])
                toz = S_oz.issue(lambda e, tok0=tok0: e.dma_start(
                    out=dtr_o[:, tok0:tok0 + PC], in_=evf[0:32, :]))
            P.wait("tensor", "sd", P.ctr["sd"])
            P.wait("tensor", "vd", P.ctr["vd"])

    S_oh.wait("sync", toh[-1])
    S_oz.wait("sync", toz)
    return P



QC = 128
HN = 4
KR = 70


def build_launch_B(Tb=T, dbg=False):
    P = Prog("B")
    nc = P.nc
    NKT = Tb // DT

    Qblk_d = nc.dram_tensor("Qblk", [HN * 64, Tb], BF16, kind="ExternalInput").ap()
    Kblk_d = nc.dram_tensor("Kblk", [HN * 64, Tb], BF16, kind="ExternalInput").ap()
    Vblk_d = nc.dram_tensor("Vblk", [Tb, HN * 65], BF16, kind="ExternalInput").ap()
    slopes4_d = nc.dram_tensor("slopes4", [1, HN], F32, kind="ExternalInput").ap()
    power_d = nc.dram_tensor("power", [DT, 1], F32, kind="ExternalInput").ap()
    khat_d = nc.dram_tensor("khat", [1, DT], F32, kind="ExternalInput").ap()
    nqhat4_d = nc.dram_tensor("nqhat4", [1, HN * QC], BF16, kind="ExternalInput").ap()
    khatt_d = nc.dram_tensor("khatt", [1, Tb], BF16, kind="ExternalInput").ap()
    ktc_d = nc.dram_tensor("ktc", [1, Tb], BF16, kind="ExternalInput").ap()
    slopes4c_d = nc.dram_tensor("slopes4c", [HN, 1], F32, kind="ExternalInput").ap()
    trimask_d = nc.dram_tensor("trimask", [DT, DT], BF16, kind="ExternalInput").ap()

    O_o = nc.dram_tensor("O", [Tb, HN * 64], BF16, kind="ExternalOutput").ap()
    if dbg:
        dbgS_o = nc.dram_tensor("dbgS", [DT, HN * 2 * DT], F32, kind="ExternalOutput").ap()
        dbgP_o = nc.dram_tensor("dbgP", [DT, HN * 2 * DT], BF16, kind="ExternalOutput").ap()
        dbgSuf_o = nc.dram_tensor("dbgSuf", [1, NKT * HN * 65], BF16, kind="ExternalOutput").ap()
        dbgTM_o = nc.dram_tensor("dbgTM", [DT, 2 * HN * 2 * DT], F32, kind="ExternalOutput").ap()
        dbgKB_o = nc.dram_tensor("dbgKB", [KR, 2 * HN * DT], BF16, kind="ExternalOutput").ap()
        dbgQB_o = nc.dram_tensor("dbgQB", [KR, HN * QC], BF16, kind="ExternalOutput").ap()
        dbgLV_o = nc.dram_tensor("dbgLV", [DT, 2 * HN * 2 * DT], F32, kind="ExternalOutput").ap()
        dbgPV_o = nc.dram_tensor("dbgPV", [DT, 2048], F32, kind="ExternalOutput").ap()

    P.register_consts([2.0, 1.0, float(EPS_ATT), -1.0, 0.0])

    kres = P.sb("kres", [72, HN * Tb], BF16)
    vres = P.sb("vres", [DT, NKT * HN * 65], BF16)
    qbuf = P.sb("qbuf", [72, 2 * HN * QC], BF16)
    stage = P.sb("stage", [HN, Tb], BF16)
    sc4_f = P.sb("sc4_f", [HN, 2])
    sc4_b = P.sb("sc4_b", [HN, 2], BF16)
    t_sb = P.sb("t_sb", [DT, 2 * HN * 2 * DT])
    m_sb = P.sb("m_sb", [DT, 3 * HN * 2 * DT])
    la_sb = P.sb("la_sb", [DT, 2 * HN * 2 * DT])
    lb_sb = P.sb("lb_sb", [DT, 2 * HN * 2 * DT])
    p_sb = P.sb("p_sb", [DT, 3 * HN * 2 * DT], BF16)
    psel_sb = P.sb("psel_sb", [DT, HN * DT], BF16)
    tri_sb = P.sb("tri_sb", [DT, DT], BF16)
    ctile_sb = P.sb("ctile_sb", [DT, DT], BF16)
    khat_sb = P.sb("khat_sb", [1, DT])
    slopes_sb = P.sb("slopes_sb", [1, HN])
    pw_sb = P.sb("pw_sb", [1, 2])
    pwcol_sb = P.sb("pwcol_sb", [DT, 1])
    crow_sb = P.sb("crow_sb", [1, DT], BF16)
    ccol_sb = P.sb("ccol_sb", [DT, 1])
    suf_sb = P.sb("suf_sb", [1, NKT * HN * 65], BF16)
    oev_sb = P.sb("oev_sb", [DT, 2 * HN * 64], BF16)
    rcol_sb = P.sb("rcol_sb", [DT, 8])
    onesr_sb = P.sb("onesr_sb", [1, DT], BF16)
    onesc_sb = P.sb("onesc_sb", [DT, 1], BF16)
    sh_b = P.sb("sh_b", [1, HN], BF16)
    sh_f = P.sb("sh_f", [1, HN])
    sl_f = P.sb("sl_f", [1, HN])
    sl_b = P.sb("sl_b", [1, HN], BF16)
    khat_b = P.sb("khat_b", [1, DT], BF16)
    onesq_f = P.sb("onesq_f", [1, QC])
    onesk_f = P.sb("onesk_f", [1, DT])
    augq_f = P.sb("augq_f", [1, 2 * HN * QC])
    augq_b = P.sb("augq_b", [1, 2 * HN * QC], BF16)
    augk_f = P.sb("augk_f", [1, 2 * HN * DT])
    augk_b = P.sb("augk_b", [1, 2 * HN * DT], BF16)

    ps_s = [P.psum("ps_s0", [DT, HN * 2 * DT]), P.psum("ps_s1", [DT, HN * 2 * DT])]
    ps_pv = P.psum("ps_pv", [DT, 2048])

    for s in ["vd", "sd", "td", "gd", "cd"]:
        P.sem(s)
    S_kr = DmaStream(P, "lkr", "sync", 1)
    S_dbgs = DmaStream(P, "ldbg", "gpsimd", 1)
    S_q = DmaStream(P, "lq", "sync", 2)
    S_v = DmaStream(P, "lv", "gpsimd", 2)
    S_p = DmaStream(P, "lp", "sync", 1)
    S_o = DmaStream(P, "go", "gpsimd", 2)

    P.emit_const_memsets("gpsimd", ("cd", 1))
    cd_init = P.ctr["cd"]

    S_p.issue(lambda e: e.dma_start(out=khat_sb[:, :], in_=khat_d))
    S_p.issue(lambda e: e.dma_start(out=slopes_sb[:, :], in_=slopes4_d))
    S_p.issue(lambda e: e.dma_start(out=pw_sb[:, 0:1], in_=power_d[0:1, :]))
    S_p.issue(lambda e: e.dma_start(out=pwcol_sb[:, :], in_=power_d))
    S_p.issue(lambda e: e.dma_start(out=tri_sb[:, :], in_=trimask_d))
    S_p.issue(lambda e: e.dma_start(out=qbuf[70:71, 0:HN * QC], in_=nqhat4_d))
    S_p.issue(lambda e: e.dma_start(out=qbuf[71:72, 0:HN * QC], in_=nqhat4_d))
    S_p.issue(lambda e: e.dma_start(out=qbuf[70:71, HN * QC:2 * HN * QC], in_=nqhat4_d))
    S_p.issue(lambda e: e.dma_start(out=qbuf[71:72, HN * QC:2 * HN * QC], in_=nqhat4_d))
    S_p.issue(lambda e: e.dma_start(out=sc4_f[:, 0:1], in_=slopes4c_d))
    for h in range(HN):
        S_p.issue(lambda e, h=h: e.dma_start(
            out=kres[66:67, h * Tb:(h + 1) * Tb], in_=khatt_d))
        S_p.issue(lambda e, h=h: e.dma_start(
            out=kres[67:68, h * Tb:(h + 1) * Tb], in_=khatt_d))
        S_p.issue(lambda e, h=h: e.dma_start(
            out=kres[68:69, h * Tb:(h + 1) * Tb], in_=ktc_d))
        toknq = S_p.issue(lambda e, h=h: e.dma_start(
            out=kres[69:70, h * Tb:(h + 1) * Tb], in_=ktc_d))
    for eng in ("vector", "scalar", "tensor"):
        S_p.wait(eng, toknq)
    for h in range(HN):
        tok_kr = S_kr.issue(lambda e, h=h: e.dma_start(
            out=kres[0:64, h * Tb:(h + 1) * Tb],
            in_=Kblk_d[h * 64:(h + 1) * 64, :]))
    tok_vr = []
    for kt in range(NKT):
        tok_vr.append(S_v.issue(lambda e, kt=kt: e.dma_start(
            out=vres[:, kt * HN * 65:(kt + 1) * HN * 65],
            in_=Vblk_d[kt * DT:(kt + 1) * DT, :]), parity=kt % 2))
    P.wait("vector", "cd", cd_init)
    P.wait("scalar", "cd", cd_init)
    P.op("vector", lambda e: e.memset(onesr_sb[:, :], 1.0), inc=("vd", 1))
    P.op("vector", lambda e: e.memset(onesc_sb[:, :], 1.0), inc=("vd", 1))
    P.op("vector", lambda e: e.memset(onesq_f[:, :], 1.0), inc=("vd", 1))
    P.op("vector", lambda e: e.memset(onesk_f[:, :], 1.0), inc=("vd", 1))
    P.op("scalar", lambda e: e.activation(
        pw_sb[0:1, 1:2], pw_sb[0:1, 0:1], AF.Exp, scale=float(LN_C0)),
        inc=("sd", 1))
    P.wait("vector", "sd", P.ctr["sd"])
    P.op("vector", lambda e: e.tensor_scalar(
        crow_sb[:, :], onesr_sb[:, :], pw_sb[0:1, 1:2], None, ALU.mult),
        inc=("vd", 1))
    P.op("vector", lambda e: e.tensor_copy(sh_b[:, :], slopes_sb[:, :]), inc=("vd", 1))
    P.op("vector", lambda e: e.tensor_copy(sh_f[:, :], sh_b[:, :]), inc=("vd", 1))
    P.op("vector", lambda e: e.tensor_tensor(
        sl_f[:, :], slopes_sb[:, :], sh_f[:, :], ALU.subtract), inc=("vd", 1))
    P.op("vector", lambda e: e.tensor_copy(sl_b[:, :], sl_f[:, :]), inc=("vd", 1))
    P.op("vector", lambda e: e.tensor_copy(sl_f[:, :], sl_b[:, :]), inc=("vd", 1))
    P.op("vector", lambda e: e.tensor_copy(khat_b[:, :], khat_sb[:, :]), inc=("vd", 1))
    for h in range(HN):
        P.op("vector", lambda e, h=h: e.tensor_scalar(
            augq_f[0:1, h * QC:(h + 1) * QC], onesq_f[:, :],
            sh_f[0:1, h:h + 1], None, ALU.mult), inc=("vd", 1))
        P.op("vector", lambda e, h=h: e.tensor_scalar(
            augq_f[0:1, HN * QC + h * QC: HN * QC + (h + 1) * QC], onesq_f[:, :],
            sl_f[0:1, h:h + 1], None, ALU.mult), inc=("vd", 1))
        P.op("vector", lambda e, h=h: e.tensor_scalar(
            augk_f[0:1, h * DT:(h + 1) * DT], onesk_f[:, :],
            sh_f[0:1, h:h + 1], None, ALU.mult), inc=("vd", 1))
        P.op("vector", lambda e, h=h: e.tensor_scalar(
            augk_f[0:1, HN * DT + h * DT: HN * DT + (h + 1) * DT], onesk_f[:, :],
            sl_f[0:1, h:h + 1], None, ALU.mult), inc=("vd", 1))
    P.op("vector", lambda e: e.tensor_copy(augq_b[:, :], augq_f[:, :]), inc=("vd", 1))
    P.op("vector", lambda e: e.tensor_copy(augk_b[:, :], augk_f[:, :]), inc=("vd", 1))
    vd_aug = P.ctr["vd"]
    P.wait("gpsimd", "vd", vd_aug)
    for qp in range(2):
        qb0 = qp * HN * QC
        P.op("gpsimd", lambda e, qb0=qb0: e.dma_start(
            out=qbuf[66:67, qb0:qb0 + HN * QC], in_=augq_b[0:1, 0:HN * QC]),
            inc=("gd", 16))
        P.op("gpsimd", lambda e, qb0=qb0: e.dma_start(
            out=qbuf[68:69, qb0:qb0 + HN * QC], in_=augq_b[0:1, 0:HN * QC]),
            inc=("gd", 16))
        P.op("gpsimd", lambda e, qb0=qb0: e.dma_start(
            out=qbuf[67:68, qb0:qb0 + HN * QC],
            in_=augq_b[0:1, HN * QC:2 * HN * QC]), inc=("gd", 16))
        P.op("gpsimd", lambda e, qb0=qb0: e.dma_start(
            out=qbuf[69:70, qb0:qb0 + HN * QC],
            in_=augq_b[0:1, HN * QC:2 * HN * QC]), inc=("gd", 16))
    P.op("vector", lambda e: e.tensor_copy(sc4_b[:, 0:1], sc4_f[:, 0:1]),
         inc=("vd", 1))
    P.op("vector", lambda e: e.tensor_copy(sc4_f[:, 1:2], sc4_b[:, 0:1]),
         inc=("vd", 1))
    P.op("vector", lambda e: e.memset(stage[:, :], 1.0), inc=("vd", 1))
    P.op("vector", lambda e: e.tensor_scalar(
        stage[:, :], stage[:, :], sc4_f[:, 1:2], None, ALU.mult), inc=("vd", 1))
    P.wait("gpsimd", "vd", P.ctr["vd"])
    for h in range(HN):
        P.op("gpsimd", lambda e, h=h: e.dma_start(
            out=kres[64:65, h * Tb:(h + 1) * Tb], in_=stage[h:h + 1, :]),
            inc=("gd", 16))
        P.op("gpsimd", lambda e, h=h: e.dma_start(
            out=kres[70:71, h * Tb:(h + 1) * Tb], in_=stage[h:h + 1, :]),
            inc=("gd", 16))
    gd_hi = P.ctr["gd"]
    P.wait("vector", "gd", gd_hi)
    P.op("vector", lambda e: e.tensor_tensor(
        sc4_f[:, 1:2], sc4_f[:, 0:1], sc4_f[:, 1:2], ALU.subtract),
        inc=("vd", 1))
    P.op("vector", lambda e: e.tensor_copy(sc4_b[:, 1:2], sc4_f[:, 1:2]),
         inc=("vd", 1))
    P.op("vector", lambda e: e.tensor_copy(sc4_f[:, 1:2], sc4_b[:, 1:2]),
         inc=("vd", 1))
    P.op("vector", lambda e: e.memset(stage[:, :], 1.0), inc=("vd", 1))
    P.op("vector", lambda e: e.tensor_scalar(
        stage[:, :], stage[:, :], sc4_f[:, 1:2], None, ALU.mult), inc=("vd", 1))
    P.wait("gpsimd", "vd", P.ctr["vd"])
    for h in range(HN):
        P.op("gpsimd", lambda e, h=h: e.dma_start(
            out=kres[65:66, h * Tb:(h + 1) * Tb], in_=stage[h:h + 1, :]),
            inc=("gd", 16))
        P.op("gpsimd", lambda e, h=h: e.dma_start(
            out=kres[71:72, h * Tb:(h + 1) * Tb], in_=stage[h:h + 1, :]),
            inc=("gd", 16))
    gd_augk = P.ctr["gd"]
    P.wait("tensor", "vd", P.ctr["vd"])
    P.wait("tensor", "gd", gd_augk)
    P.op("tensor", lambda e: e.matmul(
        ps_pv[:, 0:DT], onesr_sb[:, :], crow_sb[:, :], start=True, stop=True),
        inc=("td", 1))
    P.wait("vector", "td", P.ctr["td"])
    P.op("vector", lambda e: e.tensor_copy(ctile_sb[:, :], ps_pv[:, 0:DT]),
         inc=("vd", 1))
    P.op("vector", lambda e: e.tensor_copy(ccol_sb[:, :], ps_pv[:, 0:1]),
         inc=("vd", 1))

    run_f = P.sb("run_f", [1, HN * 65])
    P.op("vector", lambda e: e.memset(
        suf_sb[0:1, (NKT - 1) * HN * 65:NKT * HN * 65], 0.0), inc=("vd", 1))
    P.op("vector", lambda e: e.memset(run_f[:, :], 0.0), inc=("vd", 1))
    P.wait("tensor", "vd", P.ctr["vd"])
    S_v.wait("tensor", tok_vr[-2])
    S_v.wait("tensor", tok_vr[-1])
    S_kr.wait("tensor", tok_kr)
    td_pre = {}
    for idx, kt in enumerate(range(NKT - 1, 0, -1)):
        if idx >= 1:
            P.wait("tensor", "vd", P.ctr["vd"])
        P.op("tensor", lambda e, kt=kt: e.matmul(
            ps_pv[0:1, 512:512 + HN * 65],
            onesc_sb[:, :],
            vres[:, kt * HN * 65:(kt + 1) * HN * 65],
            start=True, stop=True), inc=("td", 1))
        td_pre[idx] = P.ctr["td"]
        P.wait("vector", "td", td_pre[idx])
        P.op("vector", lambda e: e.scalar_tensor_tensor(
            run_f[:, :], ps_pv[0:1, 512:512 + HN * 65], 1.0, run_f[:, :],
            ALU.mult, ALU.add), inc=("vd", 1))
        P.op("vector", lambda e, kt=kt: e.tensor_copy(
            suf_sb[0:1, (kt - 1) * HN * 65: kt * HN * 65], run_f[:, :]),
            inc=("vd", 1))
    to_hist = {}
    grp_first_kt = {}
    td_sT = {}
    vd_E2 = {}
    sd_E6 = {}
    sd_B = {}
    vd_E5 = {}
    td_pv = {}
    glob_kt = 0
    sd_norm_prev = 0
    vd_norm_prev = 0
    NW3 = HN * 2 * DT
    pendB = None
    pendC = None

    def emitB(pr):
        gi, gb2, gb3, FD = pr[0], pr[1], pr[2], pr[3]
        if gi >= 2:
            P.wait("scalar", "vd", vd_E5[gi - 2])
        P.wait("scalar", "vd", vd_E2[gi])
        P.op("scalar", lambda e, FD=FD, gb2=gb2: e.activation(
            la_sb[:, gb2:gb2 + FD], t_sb[:, gb2:gb2 + FD], AF.Ln, bias=1.0),
            inc=("sd", 1))
        P.op("scalar", lambda e, FD=FD, gb2=gb2, gb3=gb3: e.activation(
            lb_sb[:, gb2:gb2 + FD], m_sb[:, gb3:gb3 + FD], AF.Ln, bias=2.0,
            scale=2.0), inc=("sd", 1))
        sd_B[gi] = P.ctr["sd"]
        P.wait("vector", "sd", P.ctr["sd"])
        P.op("vector", lambda e, FD=FD, gb2=gb2, gb3=gb3: e.tensor_tensor(
            m_sb[:, gb3:gb3 + FD], la_sb[:, gb2:gb2 + FD],
            lb_sb[:, gb2:gb2 + FD], ALU.subtract), inc=("vd", 1))
        vd_E5[gi] = P.ctr["vd"]

    def emitC1(pr):
        gi, gb2, gb3, FD, kts, nh2, qc_, kt0_, g0 = pr
        P.wait("scalar", "vd", vd_E5[gi])
        if gi >= 3:
            P.wait("scalar", "td", td_pv[grp_first_kt[gi - 2] - 1])
        P.op("scalar", lambda e, FD=FD, gb3=gb3: e.activation(
            p_sb[:, gb3:gb3 + FD], m_sb[:, gb3:gb3 + FD], AF.Exp,
            scale=pwcol_sb[:, 0:1]), inc=("sd", 1))
        sd_E6[gi] = P.ctr["sd"]

    def emitC2(pr):
        gi, gb2, gb3, FD, kts, nh2, qc_, kt0_, g0 = pr
        vd_sel = None
        for ki, kt in enumerate(kts):
            if kt == qc_:
                P.wait("vector", "sd", sd_E6[gi])
                for h in range(HN):
                    col = gb3 + (h * nh2 + ki) * DT
                    P.op("vector", lambda e, col=col, h=h: e.scalar_tensor_tensor(
                        psel_sb[:, h * DT:(h + 1) * DT],
                        p_sb[:, col:col + DT], ccol_sb[:, 0:1], tri_sb[:, :],
                        ALU.subtract, ALU.mult), inc=("vd", 1))
                    P.op("vector", lambda e, h=h: e.tensor_scalar(
                        psel_sb[:, h * DT:(h + 1) * DT],
                        psel_sb[:, h * DT:(h + 1) * DT], ccol_sb[:, 0:1],
                        None, ALU.add), inc=("vd", 1))
                vd_sel = P.ctr["vd"]
        P.wait("tensor", "sd", sd_E6[gi])
        if vd_sel is not None:
            P.wait("tensor", "vd", vd_sel)
        if qc_ > 0 and kt0_ == 0:
            P.wait("tensor", "sd", sd_norm_prev)
            P.wait("tensor", "vd", vd_norm_prev)
        for ki, kt in enumerate(kts):
            for h in range(HN):
                col = gb3 + (h * nh2 + ki) * DT
                if kt == qc_:
                    P.op("tensor", lambda e, h=h, kt=kt: e.matmul(
                        ps_pv[:, 512 * h:512 * h + 65],
                        psel_sb[:, h * DT:(h + 1) * DT],
                        vres[:, kt * HN * 65 + 65 * h: kt * HN * 65 + 65 * h + 65],
                        start=(kt == 0), stop=False), inc=("td", 1))
                else:
                    P.op("tensor", lambda e, h=h, col=col, kt=kt: e.matmul(
                        ps_pv[:, 512 * h:512 * h + 65],
                        p_sb[:, col:col + DT],
                        vres[:, kt * HN * 65 + 65 * h: kt * HN * 65 + 65 * h + 65],
                        start=(kt == 0), stop=False), inc=("td", 1))
            td_pv[g0 + ki] = P.ctr["td"]

    td_qend = {}
    for qc in range(Tb // QC):
        q0 = qc * QC
        qb0 = (qc % 2) * HN * QC
        if qc >= 2:
            P.wait("sync", "td", td_qend[qc - 2])
        toq = None
        for h in range(HN):
            toq = S_q.issue(lambda e, h=h, q0=q0, qb0=qb0: e.dma_start(
                out=qbuf[0:64, qb0 + h * QC: qb0 + (h + 1) * QC],
                in_=Qblk_d[h * 64:(h + 1) * 64, q0:q0 + QC]), parity=qc % 2)
        S_q.wait("tensor", toq)
        if qc >= 2:
            P.wait("vector", "td", td_qend[qc - 2])
        P.op("vector", lambda e, qc=qc, qb0=qb0: e.memset(
            qbuf[64:66, qb0:qb0 + HN * QC], float(-128.0 * qc)), inc=("vd", 1))
        vd_qcc = P.ctr["vd"]
        nk = qc + 1
        for kt0 in range(0, nk, 2):
            kts = [kt0] if kt0 + 1 >= nk else [kt0, kt0 + 1]
            nh2 = len(kts)
            FD = HN * nh2 * DT
            gi = glob_kt // 2
            gb2 = (gi % 2) * NW3
            gb3 = (gi % 3) * NW3
            pss = ps_s[gi % 2]
            grp_first_kt[gi] = glob_kt
            g0 = glob_kt
            if pendC is not None:
                emitC1(pendC)
            P.wait("tensor", "vd", vd_qcc)
            for ki, kt in enumerate(kts):
                if gi >= 2 and ki == 0:
                    P.wait("tensor", "vd", vd_E2[gi - 2])
                for h in range(HN):
                    col = (h * nh2 + ki) * DT
                    P.op("tensor", lambda e, h=h, kt=kt, pss=pss, col=col,
                         qb0=qb0: e.matmul(
                        pss[:, col:col + DT],
                        kres[:, h * Tb + kt * DT: h * Tb + (kt + 1) * DT],
                        qbuf[:, qb0 + h * QC: qb0 + (h + 1) * QC],
                        start=True, stop=True), inc=("td", 1))
                td_sT[glob_kt] = P.ctr["td"]
                glob_kt += 1
            td_qend[qc] = P.ctr["td"]
            td_st_grp = td_sT[glob_kt - 1]
            P.wait("vector", "td", td_st_grp)
            if gi >= 2:
                P.wait("vector", "sd", sd_B[gi - 2])
            if gi >= 3:
                P.wait("vector", "sd", sd_E6[gi - 3])
            P.op("vector", lambda e, pss=pss, FD=FD, gb2=gb2: e.tensor_scalar(
                t_sb[:, gb2:gb2 + FD], pss[:, 0:FD], 2.0, 0.0,
                ALU.mult, ALU.max), inc=("vd", 1))
            P.op("vector", lambda e, pss=pss, FD=FD, gb2=gb2, gb3=gb3:
                 e.scalar_tensor_tensor(
                     m_sb[:, gb3:gb3 + FD], pss[:, 0:FD], -1.0,
                     t_sb[:, gb2:gb2 + FD], ALU.mult, ALU.add), inc=("vd", 1))
            vd_E2[gi] = P.ctr["vd"]
            newC = None
            if pendB is not None:
                emitB(pendB)
                newC = pendB
                pendB = None
            if pendC is not None:
                emitC2(pendC)
                pendC = None
            pendC = newC
            pendB = (gi, gb2, gb3, FD, tuple(kts), nh2, qc, kt0, g0)
        newC = None
        if pendB is not None:
            emitB(pendB)
            newC = pendB
            pendB = None
        if pendC is not None:
            emitC1(pendC)
            emitC2(pendC)
            pendC = None
        if newC is not None:
            emitC1(newC)
            emitC2(newC)
        su = qc
        opar = su % 2
        for h in range(HN):
            P.op("tensor", lambda e, h=h, su=su: e.matmul(
                ps_pv[:, 512 * h:512 * h + 65],
                crow_sb[:, :],
                suf_sb[0:1, su * HN * 65 + 65 * h: su * HN * 65 + 65 * h + 65],
                start=False, stop=True), inc=("td", 1))
        td_tail = P.ctr["td"]
        P.wait("vector", "td", td_tail)
        P.wait("vector", "sd", P.ctr["sd"])
        if su >= 2:
            S_o.wait("scalar", to_hist[su - 2])
        for h in range(HN):
            P.op("vector", lambda e, h=h: e.tensor_scalar(
                rcol_sb[:, 2 * h:2 * h + 1],
                ps_pv[:, 512 * h + 64:512 * h + 65],
                float(EPS_ATT), None, ALU.add), inc=("vd", 1))
            P.op("vector", lambda e, h=h: e.reciprocal(
                rcol_sb[:, 2 * h + 1:2 * h + 2], rcol_sb[:, 2 * h:2 * h + 1]),
                inc=("vd", 1))
            vd_rc = P.ctr["vd"]
            P.wait("scalar", "vd", vd_rc)
            P.op("scalar", lambda e, h=h, opar=opar: e.activation(
                oev_sb[:, opar * HN * 64 + 64 * h: opar * HN * 64 + 64 * h + 64],
                ps_pv[:, 512 * h:512 * h + 64],
                AF.Copy, scale=rcol_sb[:, 2 * h + 1:2 * h + 2]), inc=("sd", 1))
        sd_norm_prev = P.ctr["sd"]
        vd_norm_prev = P.ctr["vd"]
        P.wait("gpsimd", "sd", sd_norm_prev)
        to_hist[su] = S_o.issue(lambda e, su=su, opar=opar: e.dma_start(
            out=O_o[su * DT:(su + 1) * DT, :],
            in_=oev_sb[:, opar * HN * 64:(opar + 1) * HN * 64]), parity=opar)

    S_o.wait("sync", to_hist[Tb // QC - 1])
    S_o.wait("sync", to_hist[Tb // QC - 2])
    return P


def kernel(**inputs):
    raise NotImplementedError



_PROGS = {}


def _prog(name, builder):
    if name not in _PROGS:
        _PROGS[name] = builder().finalize()
    return _PROGS[name]


def kernel(**inputs):
    hp = host_prep(inputs)
    cores = list(range(NCORE))
    NKT = T // 128

    ncA = _prog("A", build_launch_A)
    inA = [{
        "xT": hp["xT"][c], "qT": hp["qT"], "srepT": hp["srepT"],
        "fbias": hp["fbias"], "laT": hp["laT"], "lbT": hp["lbT"],
        "ln_g_cols": hp["ln_g_cols"], "ln_b_cols": hp["ln_b_cols"],
        "WqT": hp["WqT"], "WkT": hp["WkT"], "WvT": hp["WvT"],
        "slopes": hp["slopes"], "kidx16": hp["kidx16"][c],
        "ones16": hp["ones16"],
    } for c in cores]
    rA = run_bass_kernel_spmd(ncA, inA, core_ids=cores).results
    x1T = [np.asarray(rA[c]["x1T"]) for c in cores]
    Qf = [np.hstack([np.asarray(rA[4 * b + q]["Q"]) for q in range(4)])
          for b in range(2)]
    Kf = [np.hstack([np.asarray(rA[4 * b + q]["K"]) for q in range(4)])
          for b in range(2)]
    Vf = [np.vstack([np.asarray(rA[4 * b + q]["V"]) for q in range(4)])
          for b in range(2)]
    wdeq = np.asarray(rA[0]["wdeq"])

    ncB = _prog("B", lambda: build_launch_B(T))
    khat = np.arange(128, dtype=np.float32).reshape(1, 128)
    rr_ = np.arange(128)[:, None]
    cc_ = np.arange(128)[None, :]
    trimask = (rr_ <= cc_).astype(np.float32).astype(BF)
    nqhat4 = np.tile(-np.arange(QC, dtype=np.float32), 4).reshape(1, 4 * QC).astype(BF)
    dtab = np.ascontiguousarray(np.broadcast_to(
        (np.arange(NKT, dtype=np.float32) * (-128.0))[:, None, None, None],
        (NKT, 2, 8, 128)).astype(BF))
    power128 = f32c(np.broadcast_to(
        np.asarray(inputs["L0_attn_power"], np.float32).reshape(1, 1), (128, 1)))
    slopes = f32c(inputs["L0_attn_alibi_slopes"]).reshape(16)
    inB = []
    for c in cores:
        b, g = c // 4, c % 4
        Qblk = np.zeros((256, T), BF)
        Kblk = np.zeros((256, T), BF)
        Vblk = np.zeros((T, 260), BF)
        for hl in range(4):
            h = 4 * g + hl
            Qblk[64 * hl:64 * hl + 64] = Qf[b][64 * h:64 * h + 64]
            Kblk[64 * hl:64 * hl + 64] = Kf[b][64 * h:64 * h + 64]
            Vblk[:, 65 * hl:65 * hl + 64] = Vf[b][:, 64 * h:64 * h + 64]
            Vblk[:, 65 * hl + 64] = np.ones(T, np.float32).astype(BF)
        inB.append(dict(
            Qblk=Qblk, Kblk=Kblk, Vblk=Vblk,
            slopes4=f32c(slopes[4 * g:4 * g + 4].reshape(1, 4)),
            slopes4c=f32c(slopes[4 * g:4 * g + 4].reshape(4, 1)),
            power=power128, khat=khat, trimask=trimask, nqhat4=nqhat4,
            khatt=np.tile(np.arange(128, dtype=np.float32), T // 128
                          ).reshape(1, T).astype(BF),
            ktc=np.repeat(np.arange(T // 128, dtype=np.float32) * 128.0, 128
                          ).reshape(1, T).astype(BF)))
    rB = run_bass_kernel_spmd(ncB, inB, core_ids=cores).results
    Ob = [np.asarray(rB[c]["O"]) for c in cores]

    ncC = _prog("C", build_launch_C)
    inC = []
    for c in cores:
        b, qt = c // 4, c % 4
        sl = slice(qt * CT, (qt + 1) * CT)
        OT = np.zeros((D, CT), BF)
        for g in range(4):
            for hl in range(4):
                h = 4 * g + hl
                OT[64 * h:64 * h + 64] = Ob[4 * b + g][sl, 64 * hl:64 * hl + 64].T
        inC.append({
            "x1T": x1T[c], "OT": OT, "WoT": hp["WoT"], "wdeq": wdeq,
            "fbias": hp["fbias"], "laT": hp["laT"], "lbT": hp["lbT"],
            "ln_g_cols": hp["ln_g_cols"], "ln_b_cols": hp["ln_b_cols"],
        })
    rC = run_bass_kernel_spmd(ncC, inC, core_ids=cores).results
    x3T = [np.asarray(rC[c]["x3T"]) for c in cores]
    xh3T = [np.asarray(rC[c]["xh3fT"]) for c in cores]
    partC = np.stack([np.asarray(rC[c]["part"])[:, 0] for c in cores])

    ncD = _prog("D", build_launch_L1FFN)
    ffn_keys = ["cn_w1T", "cn_w2T", "cn_w3T", "cn_w4c", "gc_w1T", "gc_w2T",
                "gc_w3c", "cn_b1c", "cn_b2c", "cn_b3c", "cn_b4c", "gc_b1c",
                "gc_b2c", "gc_b3c"]
    inD = []
    for c in cores:
        b = c // 4
        m = {"xT": x3T[c], "xhfT": xh3T[c],
             "parts4": f32c(partC[4 * b:4 * b + 4]), "ar": hp["ar"]}
        for nm in ffn_keys:
            m[nm] = hp[nm]
        inD.append(m)
    rD = run_bass_kernel_spmd(ncD, inD, core_ids=cores).results
    x4T = [np.asarray(rD[c]["xoutT"]) for c in cores]

    ncE = _prog("E", build_launch_E)
    inE = [{"xT": x4T[c], "WinT": hp["WinT"],
            "m_ln_w_c": hp["m_ln_w_c"], "m_ln_b_c": hp["m_ln_b_c"]}
           for c in cores]
    rE = run_bass_kernel_spmd(ncE, inE, core_ids=cores).results
    xh4T = [np.asarray(rE[c]["xh4fT"]) for c in cores]
    zf = [np.hstack([np.asarray(rE[4 * b + q]["zT"]) for q in range(4)])
          for b in range(2)]
    xbcf = [np.hstack([np.asarray(rE[4 * b + q]["xBCT"]) for q in range(4)])
            for b in range(2)]
    dtf_ = [np.hstack([np.asarray(rE[4 * b + q]["dtr"]) for q in range(4)])
            for b in range(2)]

    ncF = _prog("F", build_launch_F)
    NCH = T // 128
    inF = []
    for c in cores:
        b, hg = c // 4, c % 4
        xbcg = np.zeros((640, T + 3), BF)
        xbcg[0:512, 3:] = xbcf[b][512 * hg:512 * (hg + 1)]
        xbcg[512:640, 3:] = xbcf[b][2048:2176]
        dtg = np.ascontiguousarray(
            dtf_[b][8 * hg:8 * hg + 8].reshape(8, NCH, 128)
            .transpose(2, 1, 0).reshape(128, NCH * 8).astype(np.float32))
        inF.append({
            "zg": np.ascontiguousarray(zf[b][512 * hg:512 * (hg + 1)]),
            "xbcg": xbcg, "dtg": dtg,
            "dtbias_t": hp["dtbias_t"][hg], "alog_rep": hp["alog_rep"][hg],
            "alog8r": hp["alog8r"][hg], "dcol8": hp["dcol8"][hg],
            "convdx": hp["convdx"][hg], "convdb": hp["convdb"],
            "convdc": hp["convdc"], "convbx": hp["convbx"][hg],
            "convbb": hp["convbb"], "convbc": hp["convbc"],
            "triu128": hp["triu128"], "iden128": hp["iden128"],
            "i8t": hp["i8t"]})
    rF = run_bass_kernel_spmd(ncF, inF, core_ids=cores).results
    ygf = [np.vstack([np.asarray(rF[4 * b + g]["ygT"]) for g in range(4)])
           for b in range(2)]
    ssqf = [np.stack([np.asarray(rF[4 * b + g]["ssq"]).T.reshape(T)
                      for g in range(4)]) for b in range(2)]

    ncFw = _prog("Fw", build_launch_Fw)
    inFw = []
    for c in cores:
        b, qt = c // 4, c % 4
        sl = slice(qt * CT, (qt + 1) * CT)
        inFw.append({
            "x4T": x4T[c], "xh4fT": xh4T[c],
            "ygTf": np.ascontiguousarray(ygf[b][:, sl]),
            "ssq4": np.ascontiguousarray(ssqf[b][:, sl]),
            "norm_w_c": hp["norm_w_c"], "WoutT": hp["WoutT"]})
    rFw = run_bass_kernel_spmd(ncFw, inFw, core_ids=cores).results
    x5T = [np.asarray(rFw[c]["x5T"]) for c in cores]
    xh5T = [np.asarray(rFw[c]["xh5fT"]) for c in cores]
    partF = np.stack([np.asarray(rFw[c]["part"])[:, 0] for c in cores])

    inG = []
    for c in cores:
        b = c // 4
        m = {"xT": x5T[c], "xhfT": xh5T[c],
             "parts4": f32c(partF[4 * b:4 * b + 4]), "ar": hp["ar"]}
        for nm in ffn_keys:
            m[nm] = hp[nm]
        inG.append(m)
    rG = run_bass_kernel_spmd(_prog("D", build_launch_L1FFN), inG,
                              core_ids=cores).results

    out = np.empty((B, T, D), np.float32)
    for c in cores:
        b, qt = c // 4, c % 4
        out[b, qt * CT:(qt + 1) * CT] = np.asarray(rG[c]["xoutT"]).astype(np.float32).T
    return out
